# version 12
# baseline (speedup 1.0000x reference)
import numpy as np
import ml_dtypes

import concourse.bass as bass
import concourse.mybir as mybir
from concourse.bass_utils import run_bass_kernel_spmd

F32 = mybir.dt.float32
BF16 = mybir.dt.bfloat16
I32 = mybir.dt.int32
ALU = mybir.AluOpType
ACTF = mybir.ActivationFunctionType

NCORES = 8
B, S, D, FF = 16, 2048, 1024, 4096
SEQ_PER_CORE = B // NCORES
T = SEQ_PER_CORE * S
DC = D // 128
FC = FF // 128
EXP_SHIFT = -10.0

STAGES = (1, 2, 3, 4, 5, 6)
DBG_NT_LIMIT = []
DBG_OM = []

ENGS = ("tensor", "vector", "scalar", "gpsimd", "sync")


def pack_w_inmajor(w):
    K, N = w.shape
    return np.ascontiguousarray(
        w.reshape(K // 128, 128, N // 128, 128).transpose(2, 1, 0, 3).reshape(N // 128, 128, K))


def pack_w_outmajor(w):
    N, K = w.shape
    return np.ascontiguousarray(
        w.reshape(N // 128, 128, K // 128, 128).transpose(0, 3, 2, 1).reshape(N // 128, 128, K))


def pack_vec(v):
    n = v.shape[0] // 128
    return np.ascontiguousarray(v.reshape(n, 128).T)


def pack_attn_outmajor(w):
    N, K = w.shape
    return np.ascontiguousarray(w.reshape(N, K // 128, 128).transpose(1, 2, 0))


def pack_attn_inmajor(w):
    K, N = w.shape
    return np.ascontiguousarray(w.reshape(K // 128, 128, N))



class Sem:
    def __init__(self, nc, name):
        self.h = nc.alloc_semaphore(name)
        self.n = 0


class Prog:
    def __init__(self, nc):
        self.nc = nc
        self.streams = {e: [] for e in ENGS}
        self.mm = Sem(nc, "mm")
        self.evA = Sem(nc, "evA")
        self.evV = Sem(nc, "evV")
        self.sems = [self.mm, self.evA, self.evV]

    def emit(self, eng, fn):
        self.streams[eng].append(fn)

    def wait(self, eng, sem, val=None):
        v = sem.n if val is None else val
        if v <= 0:
            return
        self.emit(eng, lambda e, s=sem.h, v=v: e.wait_ge(s, v))

    def ins(self, eng, build, inc=None):
        if eng == "vector":
            own = self.evV
        elif eng == "scalar":
            own = self.evA
        else:
            own = None
        if own is not None:
            if own.n > 0:
                self.emit(eng, lambda e, s=own.h, v=own.n: e.wait_ge(s, v))
            if inc is None:
                inc = (own, 1)
            else:
                assert inc[0] is own, "vector/scalar ops must inc their own sem"
        if inc is not None:
            inc[0].n += inc[1]
            h, k = inc[0].h, inc[1]
            self.emit(eng, lambda e, build=build, h=h, k=k: build(e).then_inc(h, k))
        else:
            self.emit(eng, lambda e, build=build: build(e))

    def fence(self, skip=()):
        snap = [(s.h, s.n) for s in self.sems if s.n > 0]
        for dq in getattr(self, "dqs", []):
            if dq in skip:
                continue
            snap += [(s.h, s.n) for s in dq.sems if s.n > 0]
        for eng in ENGS:
            for h, v in snap:
                self.emit(eng, lambda e, h=h, v=v: e.wait_ge(h, v))


class Dq:
    def __init__(self, p, eng, name, depth=4):
        self.p, self.eng = p, eng
        self.sems = [Sem(p.nc, f"{name}{i}") for i in range(depth)]
        self.depth = depth
        self.i = 0
        p.dqs = getattr(p, "dqs", [])
        p.dqs.append(self)

    def dma(self, build):
        s = self.sems[self.i % self.depth]
        self.p.wait(self.eng, s, s.n)
        self.p.ins(self.eng, build, inc=(s, 16))
        self.i += 1
        return (s, s.n)

    def wait(self, eng, tok):
        self.p.wait(eng, tok[0], tok[1])

    def drain(self, eng):
        for s in self.sems:
            self.p.wait(eng, s, s.n)


class Arena:
    def __init__(self, nc, base, size):
        self.nc = nc
        self.base = base
        self.size = size
        self.count = 0

    def reserve_tail(self, specs):
        self.count += 1
        need = 0
        for name, shape, dtype in specs:
            need = ((need + 63) & ~63) + shape[1] * mybir.dt.size(dtype)
        need = (need + 63) & ~63
        self.size -= need
        out = {}
        off = self.size
        for name, shape, dtype in specs:
            off = (off + 63) & ~63
            out[name] = self.nc.alloc_sbuf_tensor_at(
                f"ar{self.count}_{name}", shape, dtype, offset=self.base + off)
            off += shape[1] * mybir.dt.size(dtype)
        out["_need"] = need
        return out

    def release_tail(self, need):
        self.size += need

    def layout(self, specs):
        self.count += 1
        out = {}
        off = 0
        for name, shape, dtype in specs:
            nbytes = shape[1] * mybir.dt.size(dtype)
            off = (off + 63) & ~63
            t = self.nc.alloc_sbuf_tensor_at(
                f"ar{self.count}_{name}", shape, dtype, offset=self.base + off)
            out[name] = t
            off += nbytes
        assert off <= self.size, f"arena overflow: {off} > {self.size}"
        return out



class Build:
    def __init__(self, stages):
        self.stages = stages
        nc = self.nc = bass.Bass("TRN2", target_bir_lowering=False)

        ap = self.ap = {}

        def dram_in(name, shape, dtype=F32):
            ap[name] = nc.dram_tensor(name, list(shape), dtype, kind="ExternalInput").ap()

        dram_in("xT", (DC, 128, T))
        dram_in("wup_p", (FC, 128, D))
        dram_in("wdn_p", (DC, 128, FF))
        dram_in("w1_p", (FC, 128, D))
        dram_in("w2_p", (DC, 128, FF))
        dram_in("wq_p", (DC, 128, D))
        dram_in("wk_p", (DC, 128, D))
        dram_in("wv_p", (DC, 128, D))
        dram_in("Wq_p", (DC, 128, D))
        dram_in("Wk_p", (DC, 128, D))
        dram_in("Wv_p", (DC, 128, D))
        dram_in("Wo_p", (DC, 128, 2 * D))
        dram_in("fc1_p", (FC, 128, D), I32)
        dram_in("fc2_p", (DC, 128, FF), I32)
        dram_in("fc1s_p", (FC, 128, D))
        dram_in("fc2s_p", (DC, 128, FF))
        dram_in("vecs", (128, 160))
        dram_in("rowsin", (1, 1024))
        dram_in("cpool", (128, 16))
        dram_in("identf", (128, 128))
        dram_in("identb", (128, 128), BF16)
        dram_in("onesf", (128, 128))
        dram_in("onesb", (128, 128), BF16)
        dram_in("maskb", (128, 128), BF16)
        self.yT = nc.dram_tensor("yT", [DC, 128, T], F32, kind="ExternalOutput").ap()

        def scratch(name, shape, dtype):
            ap[name] = nc.dram_tensor(name, list(shape), dtype, kind="Internal").ap()

        scratch("s_wup", (FC, 128, D), BF16)
        scratch("s_wdn", (DC, 128, FF), BF16)
        scratch("s_w1", (FC, 128, D), BF16)
        scratch("s_w2", (DC, 128, FF), BF16)
        scratch("s_wq", (DC, 128, D), BF16)
        scratch("s_wk", (DC, 128, D), BF16)
        scratch("s_wv", (DC, 128, D), BF16)
        scratch("s_Wq", (DC, 128, D), BF16)
        scratch("s_Wk", (DC, 128, D), BF16)
        scratch("s_Wv", (DC, 128, D), BF16)
        scratch("s_Wo", (DC, 128, 2 * D), BF16)
        scratch("s_fc1", (FC, 128, D), BF16)
        scratch("s_fc2", (DC, 128, FF), BF16)
        scratch("s_x2", (DC, 128, T), BF16)
        scratch("s_s", (DC, 128, T), F32)
        scratch("s_dmsg", (DC, 128, T), F32)

        self.xres = nc.alloc_sbuf_tensor("xres", [128, DC * T], F32)
        self.cpool = nc.alloc_sbuf_tensor("cpoolT", [128, 16], F32)
        self.onesf = nc.alloc_sbuf_tensor("onesfT", [128, 128], F32)
        self.onesb = nc.alloc_sbuf_tensor("onesbT", [128, 128], BF16)
        self.identf = nc.alloc_sbuf_tensor("identfT", [128, 128], F32)
        self.identb = nc.alloc_sbuf_tensor("identbT", [128, 128], BF16)
        self.maskb = nc.alloc_sbuf_tensor("maskbT", [128, 128], BF16)
        self.vecs = nc.alloc_sbuf_tensor("vecsT", [128, 160], F32)
        self.rows = nc.alloc_sbuf_tensor("rowsT", [128, 1024], F32)
        self.stat = nc.alloc_sbuf_tensor("statT", [128, 64], F32)

        rem = nc.sbuf_bytes_remaining
        self.ARENA_BYTES = rem - 32
        r = nc.bump_sbuf(self.ARENA_BYTES)
        base = r[0] if isinstance(r, tuple) else r
        self.arena = Arena(nc, base, self.ARENA_BYTES)

        self.pb = [nc.alloc_psum_tensor(f"pb{i}", [128, 512], F32) for i in range(6)]
        self.bb = [nc.alloc_psum_tensor(f"bbk{i}", [128, 1024], BF16) for i in range(2)]

        self.p = Prog(nc)
        self.dqs = Dq(self.p, "sync", "qs", depth=6)
        self.dqg = Dq(self.p, "gpsimd", "qg", depth=4)

    C_LN_EPS, C_RMS_EPS, C_ATT_EPS, C_EXPB, C_QK, C_QUART, C_HALF, C_ZERO, \
        C_ONE, C_NEG128, C_INV_D, C_NINV_D, C_INV_FF, C_NINV_FF = range(14)
    V_G1, V_BE1, V_G2, V_BE2 = 0, 8, 16, 24
    V_B1, V_B2 = 32, 64
    V_LNG, V_LNB = 72, 80
    V_FC1B, V_QFC2B = 88, 120
    V_BQ, V_BK = 128, 136

    def pump_side(self, n):
        while n > 0 and self.side_work:
            try:
                next(self.side_work[0])
                n -= 1
            except StopIteration:
                self.side_work.pop(0)

    def drain_side(self):
        while self.side_work:
            try:
                next(self.side_work[0])
            except StopIteration:
                self.side_work.pop(0)
        if self.side_res:
            self.arena.release_tail(self.side_res)
            self.side_res = None

    def cp(self, i):
        return self.cpool[:, i:i + 1]

    def ST(self, i):
        return self.stat[:, i:i + 1]

    def emit_init(self):
        p, ap = self.p, self.ap
        ctoks = []
        for (dst, src) in ((self.cpool, "cpool"), (self.onesf, "onesf"),
                           (self.onesb, "onesb"), (self.identf, "identf"),
                           (self.identb, "identb"), (self.maskb, "maskb"),
                           (self.vecs, "vecs")):
            ctoks.append(self.dqs.dma(
                lambda e, d=dst, s=src: e.dma_start(out=d[:, :], in_=ap[s][:, :])))
        ctoks.append(self.dqs.dma(
            lambda e: e.dma_start(out=self.rows[0:1, :], in_=ap["rowsin"][:, :])))
        self.tok_x = []
        for c in range(DC):
            self.tok_x.append(self.dqs.dma(lambda e, c=c: e.dma_start(
                out=self.xres[:, c * T:(c + 1) * T], in_=ap["xT"][c, :, :])))
        for eng in ENGS:
            for tk in ctoks:
                self.dqs.wait(eng, tk)
        p.ins("vector", lambda e: e.tensor_scalar(
            self.vecs[:, self.V_QFC2B:self.V_QFC2B + 8],
            self.vecs[:, self.V_QFC2B:self.V_QFC2B + 8], 0.25, None, ALU.mult),
            inc=(p.evV, 1))

    def emit_prep(self):
        p, ap = self.p, self.ap
        stages = self.stages
        casts = []
        if 2 in stages or 7 in stages:
            casts += [("wq_p", "s_wq"), ("wk_p", "s_wk"), ("wv_p", "s_wv"),
                      ("w1_p", "s_w1"), ("w2_p", "s_w2")]
        if 5 in stages:
            casts += [("Wq_p", "s_Wq"), ("Wk_p", "s_Wk"), ("Wv_p", "s_Wv"),
                      ("Wo_p", "s_Wo")]
        self.cast_tok = {}
        for (src, dst) in casts:
            self.cast_tok[dst] = self.dqg.dma(lambda e, s=src, d=dst: e.dma_start(
                out=ap[d][:, :, :], in_=ap[s][:, :, :]))
        if 1 in stages or 3 in stages:
            self._tern(("wup_p", "s_wup", FC, D))
            p.fence()
            self._tern(("wdn_p", "s_wdn", DC, FF))
            p.fence()
        self.side_work = []
        self.side_res = None
        p.fence(skip=(self.dqg,))

    def _tern(self, which):
        src, dst, OC, IC = which
        HIC = min(IC, 1024)
        nh = IC // HIC
        p, ap, pb = self.p, self.ap, self.pb
        nel = OC * 128 * IC
        NB = 4
        ar = self.arena.layout([
            ("ti", [128, NB * HIC], F32),
            ("tsg", [128, 2 * HIC], F32),
            ("tab", [128, 2 * HIC], F32),
            ("tmk", [128, 2 * HIC], F32),
            ("to", [128, NB * HIC], BF16),
            ("acc", [128, 64], F32),
        ])
        ti, tsg, tab, tmk, to, acc = (ar[k] for k in ("ti", "tsg", "tab", "tmk", "to", "acc"))
        tiles = [(oc, hf) for oc in range(OC) for hf in range(nh)]
        n = len(tiles)
        absA = []
        for i, (oc, hf) in enumerate(tiles):
            buf = i % NB
            if i >= NB:
                p.wait("gpsimd", p.evA, absA[i - NB])
            tk = self.dqg.dma(lambda e, oc=oc, hf=hf, buf=buf: e.dma_start(
                out=ti[:, buf * HIC:(buf + 1) * HIC],
                in_=ap[src][oc, :, hf * HIC:(hf + 1) * HIC]))
            self.dqg.wait("scalar", tk)
            p.ins("scalar", lambda e, buf=buf, i=i: e.activation(
                tab[:, 0:HIC], ti[:, buf * HIC:(buf + 1) * HIC], ACTF.Abs,
                accum_out=acc[:, i:i + 1]))
            absA.append(p.evA.n)
        p.wait("tensor", p.evA, p.evA.n)
        p.ins("tensor", lambda e, n=n: e.matmul(
            pb[0][0:1, 0:n], self.onesf[:, 0:1], acc[:, 0:n],
            start=True, stop=True), inc=(p.mm, 1))
        p.wait("vector", p.mm, p.mm.n)
        p.ins("vector", lambda e, n=n: e.reduce_sum(
            acc[0:1, 0:1], pb[0][0:1, 0:n], mybir.AxisListType.X))
        p.ins("vector", lambda e: e.tensor_scalar(acc[0:1, 0:1], acc[0:1, 0:1],
                                                  0.5 / nel, None, ALU.mult))
        p.wait("tensor", p.evV, p.evV.n)
        p.ins("tensor", lambda e: e.matmul(pb[0][:, 1:2], self.onesf[0:1, :], acc[0:1, 0:1],
                                           start=True, stop=True), inc=(p.mm, 1))
        p.wait("vector", p.mm, p.mm.n)
        p.ins("vector", lambda e: e.tensor_copy(acc[:, 1:2], pb[0][:, 1:2]))
        p.wait("gpsimd", p.evA, p.evA.n)
        otoks = []
        inA = []
        mulV = []
        for i, (oc, hf) in enumerate(tiles):
            buf = i % NB
            pr = i % 2
            if i >= NB:
                p.wait("gpsimd", p.evA, inA[i - NB])
                p.wait("gpsimd", p.evV, mulV[i - 2])
            tk = self.dqg.dma(lambda e, oc=oc, hf=hf, buf=buf: e.dma_start(
                out=ti[:, buf * HIC:(buf + 1) * HIC],
                in_=ap[src][oc, :, hf * HIC:(hf + 1) * HIC]))
            self.dqg.wait("scalar", tk)
            if i >= 2:
                p.wait("scalar", p.evV, mulV[i - 2])
            p.ins("scalar", lambda e, buf=buf, pr=pr: e.activation(
                tsg[:, pr * HIC:(pr + 1) * HIC], ti[:, buf * HIC:(buf + 1) * HIC],
                ACTF.Sign))
            p.ins("scalar", lambda e, buf=buf, pr=pr: e.activation(
                tab[:, pr * HIC:(pr + 1) * HIC], ti[:, buf * HIC:(buf + 1) * HIC],
                ACTF.Abs))
            inA.append(p.evA.n)
            p.wait("vector", p.evA, p.evA.n)
            if i >= NB:
                self.dqs.wait("vector", otoks[i - NB])
            p.ins("vector", lambda e, pr=pr: e.tensor_scalar(
                tmk[:, pr * HIC:(pr + 1) * HIC], tab[:, pr * HIC:(pr + 1) * HIC],
                acc[:, 1:2], None, ALU.is_gt))
            p.ins("vector", lambda e, buf=buf, pr=pr: e.tensor_mul(
                to[:, buf * HIC:(buf + 1) * HIC], tsg[:, pr * HIC:(pr + 1) * HIC],
                tmk[:, pr * HIC:(pr + 1) * HIC]))
            mulV.append(p.evV.n)
            p.wait("sync", p.evV, p.evV.n)
            otoks.append(self.dqs.dma(lambda e, oc=oc, hf=hf, buf=buf: e.dma_start(
                out=ap[dst][oc, :, hf * HIC:(hf + 1) * HIC],
                in_=to[:, buf * HIC:(buf + 1) * HIC])))

    def _deq(self, which, ar):
        qsrc, ssrc, dst, OC, IC, HIC = which
        p, ap, pb, rows = self.p, self.ap, self.pb, self.rows
        NB = 3
        qi, qf, qs, qo = ar["qi"], ar["qf"], ar["qs"], ar["qo"]
        nh = IC // HIC
        tiles = [(oc, hf) for oc in range(OC) for hf in range(nh)]
        n = len(tiles)
        otoks = {}
        cpyA = {}
        mulV = {}
        qtok = {}
        stok = {}

        def issue(i):
            oc, hf = tiles[i]
            buf = i % NB
            pr = i % 2
            if i >= NB:
                p.wait("gpsimd", p.evA, cpyA[i - NB])
            qtok[i] = self.dqg.dma(lambda e, oc=oc, hf=hf, buf=buf: e.dma_start(
                out=qi[:, buf * HIC:(buf + 1) * HIC],
                in_=ap[qsrc][oc, :, hf * HIC:(hf + 1) * HIC]))
            if i >= 2:
                p.wait("gpsimd", p.evV, mulV[i - 2])
            stok[i] = self.dqg.dma(lambda e, oc=oc, hf=hf, pr=pr: e.dma_start(
                out=qs[:, pr * HIC:(pr + 1) * HIC],
                in_=ap[ssrc][oc, :, hf * HIC:(hf + 1) * HIC]))

        def compute(i):
            oc, hf = tiles[i]
            buf = i % NB
            pr = i % 2
            self.dqg.wait("scalar", qtok[i])
            if i >= 2:
                p.wait("scalar", p.evV, mulV[i - 2])
            p.ins("scalar", lambda e, buf=buf, pr=pr: e.activation(
                qf[:, pr * HIC:(pr + 1) * HIC], qi[:, buf * HIC:(buf + 1) * HIC],
                ACTF.Copy, bias=-128.0))
            cpyA[i] = p.evA.n
            self.dqg.wait("vector", stok[i])
            p.wait("vector", p.evA, p.evA.n)
            if i >= NB:
                self.dqg.wait("vector", otoks[i - NB])
            p.ins("vector", lambda e, buf=buf, pr=pr: e.tensor_mul(
                qo[:, buf * HIC:(buf + 1) * HIC],
                qf[:, pr * HIC:(pr + 1) * HIC],
                qs[:, pr * HIC:(pr + 1) * HIC]))
            mulV[i] = p.evV.n
            p.wait("gpsimd", p.evV, p.evV.n)
            otoks[i] = self.dqg.dma(lambda e, oc=oc, hf=hf, buf=buf: e.dma_start(
                out=ap[dst][oc, :, hf * HIC:(hf + 1) * HIC],
                in_=qo[:, buf * HIC:(buf + 1) * HIC]))

        self.dqg.drain("vector")
        self.dqg.drain("scalar")
        self.dqg.drain("gpsimd")
        issue(0)
        if n > 1:
            issue(1)
        for i in range(n):
            compute(i)
            if i + 2 < n:
                issue(i + 2)
            yield

    def emit_ln_tile(self, cfg):
        p, pb, bbk = self.p, self.pb, self.bb
        kind = cfg["kind"]
        src = cfg["src"]
        NJ = cfg["nj"]
        trash, lnT = cfg["trash"], cfg["lnT"]
        lnTf = cfg.get("lnTf")
        bf = cfg["bank_free"]
        fbp = cfg.get("fb_pair", (0, 1))
        sbase = cfg.get("stat_base", 0)
        bbctr = cfg.get("bbctr")
        ST_ = self.ST
        ST = lambda i: ST_(sbase + i)

        def bwait(eng, key):
            s, v = bf[key]
            p.wait(eng, s, v)

        for j in range(NJ):
            yield ("fwd", j)
            fb = [pb[fbp[0]], pb[fbp[1]]]
            bwait("tensor", fbp[0])
            bwait("tensor", fbp[1])
            if j == 0:
                for s_, v_ in cfg.get("fwd_extra_wait", []):
                    p.wait("tensor", s_, v_)
            for c in range(DC):
                p.ins("tensor", lambda e, c=c, j=j: e.transpose(
                    fb[c // 4][:, (c % 4) * 128:(c % 4 + 1) * 128], src(c, j),
                    self.identf[:, :]), inc=(p.mm, 1) if c == DC - 1 else None)
            fwd_mm = p.mm.n
            p.wait("scalar", p.mm, fwd_mm)
            for h2 in range(2):
                p.ins("scalar", lambda e, h2=h2: e.activation(
                    trash[:, 0:512], fb[h2][:, :], ACTF.Copy, accum_out=ST(0 + h2)))
                p.ins("scalar", lambda e, h2=h2: e.activation(
                    trash[:, 0:512], fb[h2][:, :], ACTF.Square, accum_out=ST(2 + h2)))
            stats_evA = p.evA.n
            p.wait("vector", p.evA, stats_evA)
            p.ins("vector", lambda e: e.tensor_add(ST(4), ST(0), ST(1)))
            p.ins("vector", lambda e: e.tensor_add(ST(5), ST(2), ST(3)))
            p.ins("vector", lambda e: e.tensor_scalar(ST(6), ST(4), -1.0 / D, None, ALU.mult))
            p.ins("vector", lambda e: e.tensor_scalar(ST(8), ST(5), 1.0 / D, None, ALU.mult))
            p.ins("vector", lambda e: e.tensor_mul(ST(7), ST(6), ST(6)))
            p.ins("vector", lambda e: e.tensor_sub(ST(7), ST(8), ST(7)))
            p.wait("scalar", p.evV, p.evV.n)
            p.ins("scalar", lambda e: e.activation(ST(8), ST(7), ACTF.Sqrt,
                                                   bias=self.cp(self.C_LN_EPS)))
            p.wait("vector", p.evA, p.evA.n)
            p.ins("vector", lambda e: e.reciprocal(ST(9), ST(8)))
            if kind == "l0ffn":
                p.wait("scalar", p.evV, p.evV.n)
                p.ins("scalar", lambda e: e.activation(ST(8), ST(7), ACTF.Sqrt,
                                                       bias=self.cp(self.C_ZERO)))
                p.wait("vector", p.evA, p.evA.n)
                p.ins("vector", lambda e: e.tensor_mul(ST(12), ST(8), ST(9)))
                p.ins("vector", lambda e: e.tensor_scalar(ST(12), ST(12), 1e-8, None, ALU.add))
                p.ins("vector", lambda e: e.reciprocal(ST(12), ST(12)))
                p.ins("vector", lambda e: e.tensor_mul(ST(10), ST(9), ST(12)))
                ascale = 10
            elif kind == "l1ffn":
                p.ins("vector", lambda e: e.tensor_mul(ST(12), ST(9), ST(9)))
                p.ins("vector", lambda e: e.tensor_mul(ST(12), ST(12), ST(7)))
                p.wait("scalar", p.evV, p.evV.n)
                p.ins("scalar", lambda e: e.activation(ST(12), ST(12), ACTF.Sqrt,
                                                       bias=self.cp(self.C_LN_EPS)))
                p.wait("vector", p.evA, p.evA.n)
                p.ins("vector", lambda e: e.reciprocal(ST(11), ST(12)))
                ascale = 9
            else:
                ascale = 9
            if kind in ("l1ffn", "l0attn"):
                for h2 in range(2):
                    p.ins("vector", lambda e, h2=h2: e.tensor_scalar(
                        lnTf[:, h2 * 512:(h2 + 1) * 512], fb[h2][:, :], ST(6), ST(9),
                        ALU.add, ALU.mult))
                if kind == "l1ffn":
                    for h2 in range(2):
                        p.ins("vector", lambda e, h2=h2: e.tensor_scalar(
                            lnT[:, h2 * 512:(h2 + 1) * 512],
                            lnTf[:, h2 * 512:(h2 + 1) * 512], ST(11), None, ALU.mult))
                else:
                    for h2 in range(2):
                        p.ins("vector", lambda e, h2=h2: e.tensor_copy(
                            lnT[:, h2 * 512:(h2 + 1) * 512],
                            lnTf[:, h2 * 512:(h2 + 1) * 512]))
            else:
                for h2 in range(2):
                    p.ins("vector", lambda e, h2=h2, a=ascale: e.tensor_scalar(
                        lnT[:, h2 * 512:(h2 + 1) * 512], fb[h2][:, :], ST(6), ST(a),
                        ALU.add, ALU.mult))
            bf[fbp[0]] = bf[fbp[1]] = (p.evV, p.evV.n)
            apply_evV = p.evV.n

            yield ("back", j)
            if bbctr is not None:
                bbi = bbctr[0] % 2
                bbctr[0] += 1
            else:
                bbi = j % 2
            bbkey = f"bb{bbi}"
            p.wait("tensor", p.evV, apply_evV)
            bwait("tensor", bbkey)
            for c in range(DC):
                p.ins("tensor", lambda e, c=c, bbi=bbi: e.transpose(
                    bbk[bbi][:, c * 128:(c + 1) * 128], lnT[:, c * 128:(c + 1) * 128],
                    self.identb[:, :]), inc=(p.mm, 1) if c == DC - 1 else None)
            if kind in ("l1ffn", "l0attn"):
                bwait("tensor", 4)
                bwait("tensor", 5)
                for c in range(DC):
                    p.ins("tensor", lambda e, c=c: e.transpose(
                        pb[4 + c // 4][:, (c % 4) * 128:(c % 4 + 1) * 128],
                        lnTf[:, c * 128:(c + 1) * 128], self.identf[:, :]),
                        inc=(p.mm, 1) if c == DC - 1 else None)
            back_mm = p.mm.n
            p.wait("vector", p.mm, back_mm)
            if j == 0:
                for s_, v_ in cfg.get("x2_write_wait", []):
                    p.wait("vector", s_, v_)
            if kind in ("lngb", "l0attn", "l1ffn"):
                gam, bet = cfg["gam"], cfg["bet"]
                for c in range(DC):
                    p.ins("vector", lambda e, c=c, j=j, bbi=bbi: e.tensor_scalar(
                        cfg["x2_out"](c, j), bbk[bbi][:, c * 128:(c + 1) * 128],
                        self.vecs[:, gam + c:gam + c + 1], self.vecs[:, bet + c:bet + c + 1],
                        ALU.mult, ALU.add))
            else:
                p.ins("vector", lambda e, j=j, bbi=bbi: e.tensor_copy(
                    cfg["x2_out_blk"](j),
                    bbk[bbi][:, :].rearrange("q (c w) -> q c w", c=DC)))
            bf[bbkey] = (p.evV, p.evV.n)
            if kind == "l1ffn":
                for c in range(DC):
                    p.ins("vector", lambda e, c=c, j=j: e.scalar_tensor_tensor(
                        cfg["xslice"](c, j),
                        pb[4 + c // 4][:, (c % 4) * 128:(c % 4 + 1) * 128],
                        0.5, cfg["xslice"](c, j), ALU.mult, ALU.add))
                bf[4] = bf[5] = (p.evV, p.evV.n)
            elif kind == "l0attn":
                for c in range(DC):
                    p.ins("vector", lambda e, c=c, j=j: e.tensor_copy(
                        cfg["s_out"](c, j),
                        pb[4 + c // 4][:, (c % 4) * 128:(c % 4 + 1) * 128]))
                bf[4] = bf[5] = (p.evV, p.evV.n)

    def drive_ln(self, cfg):
        for _ in self.emit_ln_tile(cfg):
            pass


    def emit_ffn(self, cfg):
        p, pb, ap = self.p, self.pb, self.ap
        xres, vecs, rows = self.xres, self.vecs, self.rows
        TT = cfg.get("tt", 512)
        NJ = TT // 128
        NT = cfg.get("nt_limit", T // TT)
        is_scratch = cfg["src"] != "xres"
        h_rms = bool(cfg.get("h_rms"))
        XW = DC * TT
        specs = [
            ("x2t", [128, 2 * XW], BF16),
            ("ht", [128, FC * TT], BF16),
            ("urng", [128, 2 * 2048], BF16),
            ("drng", [128, 3 * 1024], BF16),
            ("lnT", [128, 1024], BF16),
            ("trash", [128, 512], F32),
            ("sbst", [128, (2 * TT) if h_rms else TT], F32),
        ]
        if h_rms:
            specs.append(("hsq", [128, 2 * TT], BF16))
        if cfg["ln_kind"] in ("l1ffn", "l0attn"):
            specs.append(("lnTf", [128, 1024], F32))
        if is_scratch:
            specs.append(("dtile", [128, 2 * XW], F32))
        ar = self.arena.layout(specs)
        ar.setdefault("lnTf", None)
        ar.setdefault("hsq", None)
        x2t, ht, hsq, urng, drng = ar["x2t"], ar["ht"], ar["hsq"], ar["urng"], ar["drng"]
        sbst = ar["sbst"]

        up_w, dn_w = cfg["up_w"], cfg["dn_w"]
        bank_free = {i: (p.mm, 0) for i in range(6)}
        bank_free["bb0"] = bank_free["bb1"] = (p.mm, 0)

        def bwait(eng, key):
            s, v = bank_free[key]
            p.wait(eng, s, v)

        uslot_mm = [0, 0]
        dslot_mm = [0, 0, 0]
        ucnt = [0]
        dcnt = [0]

        def load_up(fpair):
            i = ucnt[0]
            slot = i % 2
            if i == 0 and cfg.get("up_tok"):
                self.dqs.wait("sync", cfg["up_tok"])
            if i >= 2:
                p.wait("sync", p.mm, uslot_mm[slot])
            tk = self.dqs.dma(lambda e, fpair=fpair, slot=slot: e.dma_start(
                out=urng[:, slot * 2048:(slot + 1) * 2048]
                .rearrange("q (u w) -> q u w", u=2),
                in_=up_w[2 * fpair:2 * fpair + 2, :, :].rearrange("u q w -> q u w")))
            ucnt[0] += 1
            return slot, tk

        def load_dn(d, quarter):
            i = dcnt[0]
            slot = i % 3
            if i == 0 and cfg.get("dn_tok"):
                self.dqg.wait("gpsimd", cfg["dn_tok"])
            if i >= 3:
                p.wait("gpsimd", p.mm, dslot_mm[slot])
            tk = self.dqg.dma(lambda e, d=d, quarter=quarter, slot=slot: e.dma_start(
                out=drng[:, slot * 1024:(slot + 1) * 1024],
                in_=dn_w[d, :, quarter * 1024:(quarter + 1) * 1024]))
            dcnt[0] += 1
            return slot, tk

        dt_read_done = [0] * NT
        x2_read_done = [0] * NT
        dtoks = [None] * NT
        X2R = [0] * NT

        def emit_dt_load(t):
            half = t % 2
            dt = ar["dtile"]
            if t >= 2:
                assert dt_read_done[t - 2] > 0
                p.wait("sync", p.evV, dt_read_done[t - 2])
            toks = []
            for c in range(DC):
                toks.append(self.dqs.dma(lambda e, c=c, t=t, half=half: e.dma_start(
                    out=dt[:, half * XW + c * TT: half * XW + (c + 1) * TT],
                    in_=cfg["src_ap"][c, :, t * TT:(t + 1) * TT])))
            return toks

        ST = self.ST

        def emit_ln(t):
            col0 = t * TT
            half = t % 2
            if is_scratch:
                dt = ar["dtile"]
                srcf = (lambda c, j, half=half:
                        dt[:, half * XW + c * TT + j * 128: half * XW + c * TT + (j + 1) * 128])
            else:
                srcf = (lambda c, j, col0=col0:
                        xres[:, c * T + col0 + j * 128: c * T + col0 + (j + 1) * 128])
            lncfg = {
                "kind": cfg["ln_kind"],
                "src": srcf, "nj": NJ,
                "trash": ar["trash"], "lnT": ar["lnT"], "lnTf": ar.get("lnTf"),
                "bank_free": bank_free,
                "x2_out": (lambda c, j, half=half:
                           x2t[:, half * XW + c * TT + j * 128: half * XW + c * TT + (j + 1) * 128]),
                "x2_out_blk": (lambda j, half=half: x2t[:, half * XW: (half + 1) * XW]
                               .rearrange("q (c w) -> q c w", c=DC)[:, :, j * 128:(j + 1) * 128]),
                "xslice": (lambda c, j, col0=col0:
                           xres[:, c * T + col0 + j * 128: c * T + col0 + (j + 1) * 128]),
            }
            if t >= 2:
                lncfg["x2_write_wait"] = [(p.mm, x2_read_done[t - 2])]
            if is_scratch:
                lncfg["fwd_extra_wait"] = dtoks[t]
            if cfg["ln_kind"] == "lngb":
                lncfg["gam"], lncfg["bet"] = cfg["gam"], cfg["bet"]
            if cfg["ln_kind"] == "l1ffn":
                lncfg["gam"], lncfg["bet"] = self.V_LNG, self.V_LNB
            return self.emit_ln_tile(lncfg)

        def pump(g, n=1):
            if g is None:
                return
            for _ in range(n):
                try:
                    next(g)
                except StopIteration:
                    return

        if is_scratch:
            p.wait("sync", p.evV)
            for tok in cfg.get("src_ready", []):
                p.wait("sync", tok[0], tok[1])
            dtoks[0] = emit_dt_load(0)
            if NT > 1:
                dtoks[1] = emit_dt_load(1)
        pump(emit_ln(0), 100)
        X2R[0] = p.evV.n

        up_bias = cfg.get("up_bias")
        for t in range(NT):
            col0 = t * TT
            half = t % 2
            if cfg.get("pump_side"):
                self.pump_side(4)
            sq_evA = []
            ones_mm_vals = []

            def emit_ones_mm(c):
                if c == 0:
                    bwait("tensor", 4)
                p.wait("tensor", p.evA, sq_evA[c])
                p.ins("tensor", lambda e, c=c: e.matmul(
                    pb[4][0:1, 0:TT], self.onesb[:, 0:1],
                    hsq[:, (c % 2) * TT:(c % 2 + 1) * TT],
                    start=(c == 0), stop=(c == FC - 1)),
                    inc=(p.mm, 1))
                ones_mm_vals.append(p.mm.n)

            for f in range(FC):
                if f % 2 == 0:
                    slot, wl = load_up(f // 2)
                bank = 2 + (f % 2)
                p.wait("tensor", wl[0], wl[1])
                if f == 0:
                    p.wait("tensor", p.evV, X2R[t])
                bwait("tensor", bank)
                for k in range(DC):
                    p.ins("tensor", lambda e, slot=slot, k=k, f=f, bank=bank, half=half:
                          e.matmul(
                              pb[bank][:, 0:TT],
                              urng[:, slot * 2048 + (f % 2) * D + k * 128:
                                   slot * 2048 + (f % 2) * D + (k + 1) * 128],
                              x2t[:, half * XW + k * TT: half * XW + (k + 1) * TT],
                              start=(k == 0), stop=(k == DC - 1)),
                          inc=(p.mm, 1) if k == DC - 1 else None)
                uslot_mm[slot] = p.mm.n
                p.wait("scalar", p.mm)
                if up_bias is not None:
                    p.ins("scalar", lambda e, f=f, bank=bank: e.activation(
                        ht[:, f * TT:(f + 1) * TT], pb[bank][:, 0:TT], cfg["act"],
                        bias=vecs[:, up_bias + f:up_bias + f + 1]))
                else:
                    p.ins("scalar", lambda e, f=f, bank=bank: e.activation(
                        ht[:, f * TT:(f + 1) * TT], pb[bank][:, 0:TT], cfg["act"]))
                bank_free[bank] = (p.evA, p.evA.n)
                if h_rms:
                    if f >= 2:
                        p.wait("scalar", p.mm, ones_mm_vals[f - 2])
                    p.ins("scalar", lambda e, f=f: e.activation(
                        hsq[:, (f % 2) * TT:(f % 2 + 1) * TT],
                        ht[:, f * TT:(f + 1) * TT], ACTF.Square))
                    sq_evA.append(p.evA.n)
                    if f >= 1:
                        emit_ones_mm(f - 1)
            HT_READY = bank_free[2 + ((FC - 1) % 2)]
            x2_read_done[t] = p.mm.n

            SB_READY = None
            if h_rms:
                emit_ones_mm(FC - 1)
                ones_mm = p.mm.n
                p.wait("vector", p.mm, ones_mm)
                p.ins("vector", lambda e: e.tensor_scalar(
                    rows[64:65, 0:TT], pb[4][0:1, 0:TT], 1.0 / FF, None, ALU.mult))
                bank_free[4] = (p.evV, p.evV.n)
                p.wait("scalar", p.evV)
                p.ins("scalar", lambda e: e.activation(
                    rows[64:65, 0:TT], rows[64:65, 0:TT], ACTF.Sqrt,
                    bias=self.cpool[64:65, self.C_ZERO:self.C_ZERO + 1]))
                p.wait("vector", p.evA)
                p.ins("vector", lambda e: e.tensor_scalar(
                    rows[64:65, 0:TT], rows[64:65, 0:TT], 1e-8, None, ALU.add))
                p.ins("vector", lambda e: e.reciprocal(rows[64:65, TT:2 * TT],
                                                       rows[64:65, 0:TT]))
                p.ins("vector", lambda e: e.tensor_scalar(
                    rows[64:65, TT:2 * TT], rows[64:65, TT:2 * TT], 0.5, None, ALU.mult))
                p.wait("tensor", p.evV)
                bwait("tensor", 5)
                p.ins("tensor", lambda e: e.matmul(
                    pb[5][:, 0:TT], self.onesf[64:65, :], rows[64:65, TT:2 * TT],
                    start=True, stop=True), inc=(p.mm, 1))
                p.wait("vector", p.mm)
                p.ins("vector", lambda e: e.tensor_copy(sbst[:, TT:2 * TT], pb[5][:, 0:TT]))
                bank_free[5] = (p.evV, p.evV.n)
                SB_READY = p.evV.n

            gen = emit_ln(t + 1) if t + 1 < NT else None
            pump(gen, 2)

            om = cfg["out_mode"]
            for d in range(DC):
                bank = 2 + (d % 2)
                for dq in range(4):
                    slot, wl = load_dn(d, dq)
                    p.wait("tensor", wl[0], wl[1])
                    if d == 0 and dq == 0:
                        p.wait("tensor", HT_READY[0], HT_READY[1])
                    if dq == 0:
                        bwait("tensor", bank)
                    for f in range(8):
                        ff_ = dq * 8 + f
                        p.ins("tensor", lambda e, slot=slot, f=f, ff_=ff_, bank=bank: e.matmul(
                            pb[bank][:, 0:TT],
                            drng[:, slot * 1024 + f * 128: slot * 1024 + (f + 1) * 128],
                            ht[:, ff_ * TT:(ff_ + 1) * TT],
                            start=(ff_ == 0), stop=(ff_ == FC - 1)),
                            inc=(p.mm, 1) if f == 7 else None)
                    dslot_mm[slot] = p.mm.n
                if om == "half":
                    p.wait("vector", p.mm)
                    p.wait("vector", p.evV, SB_READY)
                    p.ins("vector", lambda e, bank=bank: e.tensor_mul(
                        sbst[:, 0:TT], pb[bank][:, 0:TT], sbst[:, TT:2 * TT]))
                    p.ins("vector", lambda e, d=d, col0=col0: e.tensor_add(
                        xres[:, d * T + col0: d * T + col0 + TT],
                        xres[:, d * T + col0: d * T + col0 + TT], sbst[:, 0:TT]))
                elif om == "quarter_bias":
                    p.wait("vector", p.mm)
                    p.ins("vector", lambda e, bank=bank, d=d: e.tensor_scalar(
                        sbst[:, 0:TT], pb[bank][:, 0:TT], 0.25,
                        vecs[:, self.V_QFC2B + d:self.V_QFC2B + d + 1],
                        ALU.mult, ALU.add))
                    p.ins("vector", lambda e, d=d, col0=col0: e.tensor_add(
                        xres[:, d * T + col0: d * T + col0 + TT],
                        xres[:, d * T + col0: d * T + col0 + TT], sbst[:, 0:TT]))
                elif om == "plain_bias_plus_src":
                    p.wait("vector", p.mm)
                    p.ins("vector", lambda e, bank=bank, d=d: e.tensor_scalar(
                        sbst[:, 0:TT], pb[bank][:, 0:TT], 1.0,
                        vecs[:, self.V_B2 + d:self.V_B2 + d + 1],
                        ALU.mult, ALU.add))
                    p.ins("vector", lambda e, d=d, col0=col0: e.tensor_add(
                        xres[:, d * T + col0: d * T + col0 + TT],
                        xres[:, d * T + col0: d * T + col0 + TT], sbst[:, 0:TT]))
                    p.ins("vector", lambda e, d=d, col0=col0, half=half: e.tensor_add(
                        xres[:, d * T + col0: d * T + col0 + TT],
                        xres[:, d * T + col0: d * T + col0 + TT],
                        ar["dtile"][:, half * XW + d * TT: half * XW + (d + 1) * TT]))
                bank_free[bank] = (p.evV, p.evV.n)
                pump(gen, 2)
            pump(gen, 100)
            if t + 1 < NT:
                X2R[t + 1] = p.evV.n
            dt_read_done[t] = p.evV.n
            if is_scratch and t + 2 < NT:
                dtoks[t + 2] = emit_dt_load(t + 2)
        p.fence()


    def emit_l0ffn(self):
        self.emit_ffn({
            "ln_kind": "l0ffn", "src": "xres",
            "up_w": self.ap["s_wup"], "dn_w": self.ap["s_wdn"],
            "act": ACTF.Gelu, "up_bias": None, "h_rms": True, "out_mode": "half",
        })

    def emit_l1ffn(self):
        self.emit_ffn({
            "ln_kind": "l1ffn", "src": "xres",
            "up_w": self.ap["s_fc1"], "dn_w": self.ap["s_fc2"],
            "act": ACTF.Silu, "up_bias": self.V_FC1B, "h_rms": False,
            "out_mode": "quarter_bias",
        })

    def emit_alpha(self, kind, gam=None, bet=None, want_s=False):
        p, ap = self.p, self.ap
        W = DC * 512
        ar = self.arena.layout([
            ("x2s", [128, 2 * W], BF16),
            ("ss", [128, 2 * W], F32),
            ("lnT", [128, 2 * 1024], BF16),
            ("lnTf", [128, 2 * 1024], F32),
            ("trash", [128, 512], F32),
        ])
        x2s, ss = ar["x2s"], ar["ss"]
        bank_free = {i: (self.p.mm, 0) for i in range(6)}
        bank_free["bb0"] = bank_free["bb1"] = (self.p.mm, 0)
        bbctr = [0]
        toks = []
        out_toks = {}

        def make_gen(t):
            col0 = t * 512
            pr = t % 2
            lncfg = {
                "kind": kind, "nj": 4,
                "bank_free": bank_free,
                "fb_pair": (0, 1) if pr == 0 else (2, 3),
                "stat_base": 0 if pr == 0 else 16,
                "bbctr": bbctr,
                "src": (lambda c, j, col0=col0:
                        self.xres[:, c * T + col0 + j * 128: c * T + col0 + (j + 1) * 128]),
                "trash": ar["trash"],
                "lnT": ar["lnT"][:, pr * 1024:(pr + 1) * 1024],
                "lnTf": ar["lnTf"][:, pr * 1024:(pr + 1) * 1024],
                "x2_out": (lambda c, j, pr=pr:
                           x2s[:, pr * W + c * 512 + j * 128: pr * W + c * 512 + (j + 1) * 128]),
                "x2_out_blk": (lambda j, pr=pr: x2s[:, pr * W:(pr + 1) * W]
                               .rearrange("q (c w) -> q c w", c=DC)[:, :, j * 128:(j + 1) * 128]),
                "s_out": (lambda c, j, pr=pr:
                          ss[:, pr * W + c * 512 + j * 128: pr * W + c * 512 + (j + 1) * 128]),
            }
            if t >= 2 and (t - 2) in out_toks:
                lncfg["x2_write_wait"] = out_toks[t - 2]
            if gam is not None:
                lncfg["gam"], lncfg["bet"] = gam, bet
            return self.emit_ln_tile(lncfg)

        def pump1(g):
            try:
                next(g)
                return True
            except StopIteration:
                return False

        def emit_outs(t):
            col0 = t * 512
            pr = t % 2
            p.wait("sync", p.evV)
            tk1 = self.dqs.dma(lambda e, col0=col0, pr=pr: e.dma_start(
                out=ap["s_x2"][:, :, col0:col0 + 512].rearrange("c q w -> q c w"),
                in_=x2s[:, pr * W:(pr + 1) * W].rearrange("q (c w) -> q c w", c=DC)))
            toks.append(tk1)
            if want_s:
                tk2 = self.dqs.dma(lambda e, col0=col0, pr=pr: e.dma_start(
                    out=ap["s_s"][:, :, col0:col0 + 512].rearrange("c q w -> q c w"),
                    in_=ss[:, pr * W:(pr + 1) * W].rearrange("q (c w) -> q c w", c=DC)))
                toks.append(tk2)
                out_toks[t] = [tk1, tk2]
            else:
                out_toks[t] = [tk1]

        qd = []
        for t in range(T // 512):
            qd.append((t, make_gen(t)))
            if len(qd) == 2:
                (t0, g0), (t1, g1) = qd
                alive = True
                while alive:
                    alive = pump1(g0)
                    pump1(g1)
                emit_outs(t0)
                qd.pop(0)
        tl, gl = qd[0]
        while pump1(gl):
            pass
        emit_outs(tl)
        return toks

    def emit_l0attn(self):
        p, ap, pb = self.p, self.ap, self.pb
        vecs, onesf, onesb = self.vecs, self.onesf, self.onesb
        if 4 in self.stages or 6 in self.stages:
            HICQ = 128
            dq_ar = self.arena.reserve_tail([
                ("qi", [128, 3 * HICQ], I32),
                ("qf", [128, 2 * HICQ], F32),
                ("qs", [128, 2 * HICQ], F32),
                ("qo", [128, 3 * HICQ], BF16),
            ])
            self.side_res = dq_ar["_need"]
            self.side_work.append(self._deq(("fc1_p", "fc1s_p", "s_fc1", FC, D, HICQ), dq_ar))
            self.side_work.append(self._deq(("fc2_p", "fc2s_p", "s_fc2", DC, FF, HICQ), dq_ar))
        a_toks = self.emit_alpha("l0attn", self.V_G1, self.V_BE1, want_s=True)
        p.fence()

        dmsg_toks = []
        for s in range(SEQ_PER_CORE):
            soff = s * S
            for hg in range(2):
                ar = self.arena.layout([
                    ("kvacc", [128, 4 * 129], F32),
                    ("kvbf", [128, 4 * 129], BF16),
                    ("kbuf", [128, 16 * 512], BF16),
                    ("vbuf", [128, 16 * 512], BF16),
                    ("x2t", [128, DC * 512], BF16),
                    ("wkb", [128, DC * 512], BF16),
                    ("wvb", [128, DC * 512], BF16),
                    ("est", [128, 4 * 512], F32),
                ])
                kbuf, vbuf, x2t = ar["kbuf"], ar["vbuf"], ar["x2t"]
                wkb, wvb, est = ar["wkb"], ar["wvb"], ar["est"]
                kvacc, kvbf = ar["kvacc"], ar["kvbf"]
                p.wait("sync", p.mm)
                p.wait("sync", p.evV)
                p.wait("sync", p.evA)
                self.dqs.drain("sync")
                if "s_wk" in self.cast_tok:
                    self.dqs.wait("sync", self.cast_tok["s_wk"])
                    self.dqs.wait("sync", self.cast_tok["s_wv"])
                tkk = self.dqs.dma(lambda e, hg=hg: e.dma_start(
                    out=wkb[:, :].rearrange("q (c w) -> q c w", c=DC),
                    in_=ap["s_wk"][:, :, hg * 512:(hg + 1) * 512].rearrange("c q w -> q c w")))
                tkv = self.dqs.dma(lambda e, hg=hg: e.dma_start(
                    out=wvb[:, :].rearrange("q (c w) -> q c w", c=DC),
                    in_=ap["s_wv"][:, :, hg * 512:(hg + 1) * 512].rearrange("c q w -> q c w")))
                self.dqs.wait("tensor", tkk)
                self.dqs.wait("tensor", tkv)
                flushed = 0
                kvfree = {0: (p.mm, 0), 1: (p.mm, 0), 2: (p.mm, 0), 3: (p.mm, 0),
                          4: (p.mm, 0), 5: (p.mm, 0)}
                for tt in range(4):
                    col0 = soff + tt * 512
                    self.pump_side(12)
                    p.wait("sync", p.mm)
                    tkx = self.dqs.dma(lambda e, col0=col0: e.dma_start(
                        out=x2t[:, :].rearrange("q (c w) -> q c w", c=DC),
                        in_=ap["s_x2"][:, :, col0:col0 + 512].rearrange("c q w -> q c w")))
                    self.dqs.wait("tensor", tkx)
                    for i in range(4):
                        tc = tt * 4 + i
                        kb_bank = (2 * tc) % 4
                        vb_bank = (2 * tc + 1) % 4
                        pr = tc % 2
                        p.wait("tensor", kvfree[kb_bank][0], kvfree[kb_bank][1])
                        for c in range(DC):
                            p.ins("tensor", lambda e, c=c, i=i, kb_bank=kb_bank: e.matmul(
                                pb[kb_bank][:, :],
                                x2t[:, c * 512 + i * 128: c * 512 + (i + 1) * 128],
                                wkb[:, c * 512:(c + 1) * 512],
                                start=(c == 0), stop=(c == DC - 1)),
                                inc=(p.mm, 1) if c == DC - 1 else None)
                        kmm = p.mm.n
                        p.wait("tensor", kvfree[vb_bank][0], kvfree[vb_bank][1])
                        for c in range(DC):
                            p.ins("tensor", lambda e, c=c, i=i, vb_bank=vb_bank: e.matmul(
                                pb[vb_bank][:, :],
                                x2t[:, c * 512 + i * 128: c * 512 + (i + 1) * 128],
                                wvb[:, c * 512:(c + 1) * 512],
                                start=(c == 0), stop=(c == DC - 1)),
                                inc=(p.mm, 1) if c == DC - 1 else None)
                        vmm = p.mm.n
                        p.wait("vector", p.mm, kmm)
                        p.ins("vector", lambda e, kb_bank=kb_bank, pr=pr: e.tensor_scalar(
                            est[:, pr * 1024: pr * 1024 + 512], pb[kb_bank][:, :],
                            0.0, None, ALU.min))
                        p.wait("scalar", p.evV, p.evV.n)
                        p.ins("scalar", lambda e, pr=pr: e.activation(
                            est[:, pr * 1024 + 512: pr * 1024 + 1024],
                            est[:, pr * 1024: pr * 1024 + 512], ACTF.Exp))
                        p.wait("vector", p.evA, p.evA.n)
                        p.ins("vector", lambda e, tc=tc, kb_bank=kb_bank, pr=pr:
                              e.scalar_tensor_tensor(
                                  kbuf[:, tc * 512:(tc + 1) * 512], pb[kb_bank][:, :], 0.0,
                                  est[:, pr * 1024 + 512: pr * 1024 + 1024],
                                  ALU.max, ALU.add))
                        kvfree[kb_bank] = (p.evV, p.evV.n)
                        p.wait("vector", p.mm, vmm)
                        p.ins("vector", lambda e, tc=tc, vb_bank=vb_bank: e.tensor_copy(
                            vbuf[:, tc * 512:(tc + 1) * 512], pb[vb_bank][:, :]))
                        kvfree[vb_bank] = (p.evV, p.evV.n)
                    p.wait("tensor", p.evV)
                    for h in range(4):
                        kv_bank = 4 + (h % 2)
                        p.wait("tensor", kvfree[kv_bank][0], kvfree[kv_bank][1])
                        nmm = 0
                        for i in range(4):
                            tc = tt * 4 + i
                            for (colr, rhs) in ((slice(0, 128),
                                                 lambda e, tc=tc, h=h: vbuf[:, tc * 512 + h * 128: tc * 512 + (h + 1) * 128]),
                                                (slice(128, 129),
                                                 lambda e: onesb[:, 0:1])):
                                nmm += 1
                                p.ins("tensor", lambda e, tc=tc, h=h, colr=colr, rhs=rhs,
                                      kv_bank=kv_bank, nmm=nmm: e.matmul(
                                          pb[kv_bank][:, colr],
                                          kbuf[:, tc * 512 + h * 128: tc * 512 + (h + 1) * 128],
                                          rhs(e),
                                          start=(nmm == 1), stop=(nmm == 8)),
                                      inc=(p.mm, 1) if nmm == 8 else None)
                        p.wait("vector", p.mm)
                        if tt == 0:
                            p.ins("vector", lambda e, h=h, kv_bank=kv_bank: e.tensor_copy(
                                kvacc[:, h * 129:(h + 1) * 129], pb[kv_bank][:, 0:129]))
                        else:
                            p.ins("vector", lambda e, h=h, kv_bank=kv_bank: e.tensor_add(
                                kvacc[:, h * 129:(h + 1) * 129],
                                kvacc[:, h * 129:(h + 1) * 129], pb[kv_bank][:, 0:129]))
                        kvfree[kv_bank] = (p.evV, p.evV.n)
                p.ins("vector", lambda e: e.tensor_copy(kvbf[:, :], kvacc[:, :]))
                KV_READY = p.evV.n

                ar2 = self.arena.layout([
                    ("kvacc2", [128, 4 * 129], F32),
                    ("kvbf2", [128, 4 * 129], BF16),
                    ("qt", [128, 4 * 512], BF16),
                    ("x2q", [128, DC * 512], BF16),
                    ("wqb", [128, DC * 512], BF16),
                    ("stile", [128, 4 * 512], F32),
                    ("zq", [128, 2 * 512], F32),
                    ("zbs", [128, 2 * 512], F32),
                    ("dst", [128, 2 * 512], F32),
                    ("eq", [128, 4 * 512], F32),
                ])
                qt, x2q, wqb, stile = ar2["qt"], ar2["x2q"], ar2["wqb"], ar2["stile"]
                zq, zbs, dst, eq = ar2["zq"], ar2["zbs"], ar2["dst"], ar2["eq"]
                p.wait("sync", p.mm)
                p.wait("sync", p.evV)
                self.dqs.drain("sync")
                if "s_wq" in self.cast_tok:
                    self.dqs.wait("sync", self.cast_tok["s_wq"])
                tkw = self.dqs.dma(lambda e, hg=hg: e.dma_start(
                    out=wqb[:, :].rearrange("q (c w) -> q c w", c=DC),
                    in_=ap["s_wq"][:, :, hg * 512:(hg + 1) * 512].rearrange("c q w -> q c w")))
                self.dqs.wait("tensor", tkw)
                for lt in range(4):
                    col0 = soff + lt * 512
                    p.wait("sync", p.mm)
                    p.wait("sync", p.evV)
                    tkx = self.dqs.dma(lambda e, col0=col0: e.dma_start(
                        out=x2q[:, :].rearrange("q (c w) -> q c w", c=DC),
                        in_=ap["s_x2"][:, :, col0:col0 + 512].rearrange("c q w -> q c w")))
                    tks = self.dqs.dma(lambda e, col0=col0, hg=hg: e.dma_start(
                        out=stile[:, :].rearrange("q (c w) -> q c w", c=4),
                        in_=ap["s_s"][hg * 4:(hg + 1) * 4, :, col0:col0 + 512]
                        .rearrange("c q w -> q c w")))
                    self.dqs.wait("tensor", tkx)
                    qfree = {0: (p.mm, 0), 1: (p.mm, 0)}
                    for r in range(4):
                        bank = 0 + (r % 2)
                        p.wait("tensor", qfree[bank][0], qfree[bank][1])
                        for c in range(DC):
                            p.ins("tensor", lambda e, c=c, r=r, hg=hg, bank=bank: e.matmul(
                                pb[bank][:, :],
                                wqb[:, c * 512 + r * 128: c * 512 + (r + 1) * 128],
                                x2q[:, c * 512:(c + 1) * 512],
                                start=(c == 0), stop=(c == DC - 1)),
                                inc=(p.mm, 1) if c == DC - 1 else None)
                        qmm = p.mm.n
                        p.wait("vector", p.mm, qmm)
                        pr = r % 2
                        p.ins("vector", lambda e, bank=bank, pr=pr: e.tensor_scalar(
                            eq[:, pr * 1024: pr * 1024 + 512], pb[bank][:, :],
                            0.0, None, ALU.min))
                        p.wait("scalar", p.evV, p.evV.n)
                        p.ins("scalar", lambda e, pr=pr: e.activation(
                            eq[:, pr * 1024 + 512: pr * 1024 + 1024],
                            eq[:, pr * 1024: pr * 1024 + 512], ACTF.Exp))
                        p.wait("vector", p.evA, p.evA.n)
                        p.ins("vector", lambda e, r=r, bank=bank, pr=pr:
                              e.scalar_tensor_tensor(
                                  qt[:, r * 512:(r + 1) * 512], pb[bank][:, :], 0.0,
                                  eq[:, pr * 1024 + 512: pr * 1024 + 1024],
                                  ALU.max, ALU.add))
                        qfree[bank] = (p.evV, p.evV.n)
                    Q_READY = p.evV.n
                    self.dqs.wait("vector", tks)
                    mfree = {0: (p.evV, qfree[0][1]), 1: (p.evV, qfree[1][1]),
                             2: (p.mm, 0), 3: (p.mm, 0), 4: (p.mm, 0), 5: (p.mm, 0)}
                    pend = [None]

                    def l0_tail():
                        if pend[0] is None:
                            return
                        hh, mb, db, zb_bank, recV = pend[0]
                        pend[0] = None
                        pr = hh % 2
                        p.wait("tensor", p.evV, recV)
                        p.wait("tensor", mfree[zb_bank][0], mfree[zb_bank][1])
                        p.ins("tensor", lambda e, zb_bank=zb_bank, pr=pr: e.matmul(
                            pb[zb_bank][:, :], onesf[0:1, :], zq[0:1, pr * 512:(pr + 1) * 512],
                            start=True, stop=True), inc=(p.mm, 1))
                        p.wait("vector", p.mm, p.mm.n)
                        p.ins("vector", lambda e, zb_bank=zb_bank, pr=pr: e.tensor_copy(
                            zbs[:, pr * 512:(pr + 1) * 512], pb[zb_bank][:, :]))
                        mfree[zb_bank] = (p.evV, p.evV.n)
                        if len(dmsg_toks) >= 2:
                            self.dqs.wait("vector", dmsg_toks[-2])
                        p.ins("vector", lambda e, mb=mb, pr=pr: e.tensor_mul(
                            dst[:, pr * 512:(pr + 1) * 512], pb[mb][:, :],
                            zbs[:, pr * 512:(pr + 1) * 512]))
                        mfree[mb] = (p.evV, p.evV.n)
                        p.ins("vector", lambda e, hh=hh, pr=pr: e.tensor_add(
                            dst[:, pr * 512:(pr + 1) * 512],
                            dst[:, pr * 512:(pr + 1) * 512],
                            stile[:, hh * 512:(hh + 1) * 512]))
                        p.wait("sync", p.evV)
                        dmsg_toks.append(self.dqs.dma(
                            lambda e, hh=hh, hg=hg, col0=col0, pr=pr: e.dma_start(
                                out=ap["s_dmsg"][hg * 4 + hh, :, col0:col0 + 512],
                                in_=dst[:, pr * 512:(pr + 1) * 512])))

                    for h in range(4):
                        mb = 2 + (h % 2)
                        db = 0 + (h % 2)
                        zb_bank = 4 + (h % 2)
                        pr = h % 2
                        if h == 0:
                            p.wait("tensor", p.evV, max(KV_READY, Q_READY))
                        p.wait("tensor", mfree[mb][0], mfree[mb][1])
                        p.ins("tensor", lambda e, h=h, mb=mb: e.matmul(
                            pb[mb][:, :], kvbf[:, h * 129: h * 129 + 128],
                            qt[:, h * 512:(h + 1) * 512], start=True, stop=True),
                            inc=(p.mm, 1))
                        p.wait("tensor", mfree[db][0], mfree[db][1])
                        p.ins("tensor", lambda e, h=h, db=db: e.matmul(
                            pb[db][0:1, :], kvbf[:, h * 129 + 128: h * 129 + 129],
                            qt[:, h * 512:(h + 1) * 512], start=True, stop=True),
                            inc=(p.mm, 1))
                        mmv = p.mm.n
                        l0_tail()
                        p.wait("vector", p.mm, mmv)
                        p.ins("vector", lambda e, db=db, pr=pr: e.tensor_scalar(
                            zq[0:1, pr * 512:(pr + 1) * 512], pb[db][0:1, :],
                            1e-6, None, ALU.add))
                        mfree[db] = (p.evV, p.evV.n)
                        p.ins("vector", lambda e, pr=pr: e.reciprocal(
                            zq[0:1, pr * 512:(pr + 1) * 512],
                            zq[0:1, pr * 512:(pr + 1) * 512]))
                        pend[0] = (h, mb, db, zb_bank, p.evV.n)
                    l0_tail()
        p.fence()


        self.emit_l0attn_inner()

    def emit_l0attn_inner(self):
        self.emit_ffn({
            "ln_kind": "lngb", "gam": self.V_G2, "bet": self.V_BE2,
            "src": "scratch", "src_ap": self.ap["s_dmsg"], "src_ready": [],
            "up_tok": self.cast_tok.get("s_w1"), "dn_tok": self.cast_tok.get("s_w2"),
            "up_w": self.ap["s_w1"], "dn_w": self.ap["s_w2"],
            "act": ACTF.Gelu, "up_bias": self.V_B1, "h_rms": False,
            "pump_side": True,
            "out_mode": DBG_OM[0] if DBG_OM else "plain_bias_plus_src", "tt": 256,
            **({"nt_limit": DBG_NT_LIMIT[0]} if DBG_NT_LIMIT else {}),
        })
        self.drain_side()
        self.p.fence()

    def emit_l1attn(self):
        p, ap, pb = self.p, self.ap, self.pb
        vecs, onesf, onesb, maskb, rows = self.vecs, self.onesf, self.onesb, self.maskb, self.rows
        self.emit_alpha("plain", want_s=False)
        p.fence()

        NH = 8
        for s in range(SEQ_PER_CORE):
            soff = s * S
            for H in range(2):
                ar = self.arena.layout([
                    ("ktb", [128, 4 * 2048], BF16),
                    ("vb", [128, 16 * 520], BF16),
                    ("x2t", [128, DC * 512], BF16),
                    ("wb", [128, DC * 512], BF16),
                    ("qb", [128, 4 * 512], BF16),
                    ("msgb", [128, 8 * 512], BF16),
                    ("pt", [128, 4 * 512], BF16),
                    ("wob", [128, 2 * 1024], BF16),
                    ("zbs", [128, 512], F32),
                ])
                ktb, vb, x2t, wb, qb = ar["ktb"], ar["vb"], ar["x2t"], ar["wb"], ar["qb"]
                msgb, ptb, wob = ar["msgb"], ar["pt"], ar["wob"]
                zbs = ar["zbs"]
                zst = zbs
                p.wait("sync", p.mm)
                p.wait("sync", p.evV)
                p.wait("sync", p.evA)
                self.dqs.drain("sync")
                p.ins("vector", lambda e: e.memset(msgb[64:128, :], 0.0))

                for nm in ("s_Wk", "s_Wv", "s_Wq", "s_Wo"):
                    if nm in self.cast_tok:
                        self.dqs.wait("sync", self.cast_tok[nm])
                tkw = self.dqs.dma(lambda e, H=H: e.dma_start(
                    out=wb[:, :].rearrange("q (c w) -> q c w", c=DC),
                    in_=ap["s_Wk"][:, :, H * 512:(H + 1) * 512].rearrange("c q w -> q c w")))
                self.dqs.wait("tensor", tkw)
                kfree = {i: (p.mm, 0) for i in range(6)}
                for tt in range(4):
                    col0 = soff + tt * 512
                    p.wait("sync", p.mm)
                    tkx = self.dqs.dma(lambda e, col0=col0: e.dma_start(
                        out=x2t[:, :].rearrange("q (c w) -> q c w", c=DC),
                        in_=ap["s_x2"][:, :, col0:col0 + 512].rearrange("c q w -> q c w")))
                    self.dqs.wait("tensor", tkx)
                    for r in range(4):
                        bank = (tt * 4 + r) % 4
                        p.wait("tensor", kfree[bank][0], kfree[bank][1])
                        for c in range(DC):
                            p.ins("tensor", lambda e, c=c, r=r, bank=bank: e.matmul(
                                pb[bank][:, :],
                                wb[:, c * 512 + r * 128: c * 512 + (r + 1) * 128],
                                x2t[:, c * 512:(c + 1) * 512],
                                start=(c == 0), stop=(c == DC - 1)),
                                inc=(p.mm, 1) if c == DC - 1 else None)
                        p.wait("vector", p.mm, p.mm.n)
                        p.ins("vector", lambda e, r=r, tt=tt, H=H, bank=bank: e.tensor_scalar(
                            ktb[:, r * 2048 + tt * 512: r * 2048 + (tt + 1) * 512],
                            pb[bank][:, :], 1.0,
                            vecs[:, self.V_BK + H * 4 + r: self.V_BK + H * 4 + r + 1],
                            ALU.mult, ALU.add))
                        kfree[bank] = (p.evV, p.evV.n)
                p.wait("sync", p.mm)
                tkw = self.dqs.dma(lambda e, H=H: e.dma_start(
                    out=wb[:, :].rearrange("q (c w) -> q c w", c=DC),
                    in_=ap["s_Wv"][:, :, H * 512:(H + 1) * 512].rearrange("c q w -> q c w")))
                self.dqs.wait("tensor", tkw)
                for h in range(NH):
                    p.ins("vector", lambda e, h=h: e.memset(
                        vb[:, :].rearrange("q (tc w) -> q tc w", tc=16)[:, :, h * 65 + 64: h * 65 + 65],
                        1.0))
                for tt in range(4):
                    col0 = soff + tt * 512
                    p.wait("sync", p.mm)
                    tkx = self.dqs.dma(lambda e, col0=col0: e.dma_start(
                        out=x2t[:, :].rearrange("q (c w) -> q c w", c=DC),
                        in_=ap["s_x2"][:, :, col0:col0 + 512].rearrange("c q w -> q c w")))
                    self.dqs.wait("tensor", tkx)
                    for i in range(4):
                        tc = tt * 4 + i
                        bank = tc % 4
                        p.wait("tensor", kfree[bank][0], kfree[bank][1])
                        for c in range(DC):
                            p.ins("tensor", lambda e, c=c, i=i, bank=bank: e.matmul(
                                pb[bank][:, :],
                                x2t[:, c * 512 + i * 128: c * 512 + (i + 1) * 128],
                                wb[:, c * 512:(c + 1) * 512],
                                start=(c == 0), stop=False),
                                inc=None)
                        p.ins("tensor", lambda e, i=i, H=H, bank=bank: e.matmul(
                            pb[bank][:, :], onesf[0:1, 0:128],
                            rows[0:1, H * 512:(H + 1) * 512],
                            start=False, stop=True), inc=(p.mm, 1))
                        p.wait("vector", p.mm, p.mm.n)
                        p.ins("vector", lambda e, tc=tc, bank=bank: e.tensor_copy(
                            vb[:, tc * 520: tc * 520 + 520]
                            .rearrange("q (h w) -> q h w", h=NH)[:, :, 0:64],
                            pb[bank][:, :].rearrange("q (h w) -> q h w", h=NH)))
                        kfree[bank] = (p.evV, p.evV.n)
                p.wait("sync", p.mm)
                tkwq = self.dqs.dma(lambda e, H=H: e.dma_start(
                    out=wb[:, :].rearrange("q (c w) -> q c w", c=DC),
                    in_=ap["s_Wq"][:, :, H * 512:(H + 1) * 512].rearrange("c q w -> q c w")))
                self.dqs.wait("tensor", tkwq)
                for qt in range(4):
                    col0 = soff + qt * 512
                    p.wait("sync", p.mm)
                    tkx = self.dqs.dma(lambda e, col0=col0: e.dma_start(
                        out=x2t[:, :].rearrange("q (c w) -> q c w", c=DC),
                        in_=ap["s_x2"][:, :, col0:col0 + 512].rearrange("c q w -> q c w")))
                    self.dqs.wait("tensor", tkx)
                    for r in range(4):
                        bank = r % 4
                        p.wait("tensor", kfree[bank][0], kfree[bank][1])
                        for c in range(DC):
                            p.ins("tensor", lambda e, c=c, r=r, bank=bank: e.matmul(
                                pb[bank][:, :],
                                wb[:, c * 512 + r * 128: c * 512 + (r + 1) * 128],
                                x2t[:, c * 512:(c + 1) * 512],
                                start=(c == 0), stop=(c == DC - 1)),
                                inc=(p.mm, 1) if c == DC - 1 else None)
                        p.wait("vector", p.mm, p.mm.n)
                        p.ins("vector", lambda e, r=r, H=H, bank=bank: e.tensor_scalar(
                            qb[:, r * 512:(r + 1) * 512], pb[bank][:, :], 1.0,
                            vecs[:, self.V_BQ + H * 4 + r: self.V_BQ + H * 4 + r + 1],
                            ALU.mult, ALU.add))
                        kfree[bank] = (p.evV, p.evV.n)
                    QB_READY = p.evV.n
                    if qt <= 1:
                        kts = [(kt, 0, False) for kt in range(8)]
                    elif qt == 2:
                        kts = [(kt, 0, False) for kt in range(8)] + \
                              [(8 + i, i * 128, True) for i in range(4)]
                    else:
                        kts = [(kt, 0, False) for kt in range(12)] + \
                              [(12 + i, i * 128, True) for i in range(4)]
                    srot = []
                    avrot = {}
                    o_free = {4: (p.mm, 0), 5: (p.mm, 0)}
                    skt = [0]
                    pending = [None]

                    def emit_sbank_mm():
                        i = skt[0]
                        sb = i % 4
                        ps = i % 4
                        if i >= 4:
                            p.wait("tensor", srot[i - 4][0], srot[i - 4][1])
                        elif i == 0:
                            p.wait("tensor", p.evV, QB_READY)
                        skt[0] += 1
                        return sb, ps, i

                    def flush_tail():
                        if pending[0] is None:
                            return
                        hh, ob, rv = pending[0]
                        pending[0] = None
                        sb, _, _i = emit_sbank_mm()
                        avrot[_i] = 0
                        p.wait("tensor", p.evV, rv)
                        p.ins("tensor", lambda e, sb=sb: e.matmul(
                            pb[sb][:, :], onesf[64:65, :], zst[64:65, :],
                            start=True, stop=True), inc=(p.mm, 1))
                        p.wait("vector", p.mm, p.mm.n)
                        p.ins("vector", lambda e: e.tensor_copy(zbs[:, :], pb[sb][:, :]))
                        srot.append((p.evV, p.evV.n))
                        kfree[sb] = (p.evV, p.evV.n)
                        p.ins("vector", lambda e, hh=hh, ob=ob: e.tensor_mul(
                            msgb[0:64, hh * 512:(hh + 1) * 512],
                            pb[ob][0:64, :], zbs[0:64, :]))
                        o_free[ob] = (p.evV, p.evV.n)

                    av_q = [None]

                    def emit_av():
                        if av_q[0] is None:
                            return
                        fn = av_q[0]
                        av_q[0] = None
                        fn()

                    for h in range(NH):
                        r, bp = h // 2, (h % 2) * 64
                        ob = 4 + (h % 2)
                        av_last = [0]
                        for n_kt, (kt, live0, diag) in enumerate(kts):
                            sb, ps, ii = emit_sbank_mm()
                            p.ins("tensor", lambda e, kt=kt, live0=live0, r=r, bp=bp,
                                  sb=sb: e.matmul(
                                      pb[sb][:, live0:512],
                                      ktb[bp:bp + 64, r * 2048 + kt * 128: r * 2048 + (kt + 1) * 128],
                                      qb[bp:bp + 64, r * 512 + live0: r * 512 + 512],
                                      start=True, stop=True), inc=(p.mm, 1))
                            smm = p.mm.n
                            if n_kt == 0:
                                p.wait("tensor", o_free[ob][0], o_free[ob][1])
                            emit_av()
                            if n_kt == 2:
                                flush_tail()
                            p.wait("scalar", p.mm, smm)
                            if ii >= 4 and avrot.get(ii - 4):
                                p.wait("scalar", p.mm, avrot[ii - 4])
                            p.ins("scalar", lambda e, ps=ps, live0=live0, sb=sb:
                                  e.activation(
                                      ptb[:, ps * 512 + live0: (ps + 1) * 512],
                                      pb[sb][:, live0:512], ACTF.Exp,
                                      bias=self.cp(self.C_EXPB), scale=self.cp(self.C_QK)))
                            srot.append((p.evA, p.evA.n))
                            exp_evA = p.evA.n
                            mask_evV = None
                            if diag:
                                p.wait("vector", p.evA, exp_evA)
                                p.ins("vector", lambda e, ps=ps, live0=live0: e.tensor_mul(
                                    ptb[:, ps * 512 + live0: ps * 512 + live0 + 128],
                                    ptb[:, ps * 512 + live0: ps * 512 + live0 + 128],
                                    maskb[:, :]))
                                mask_evV = p.evV.n

                            def mk_av(kt=kt, h=h, ps=ps, live0=live0, n_kt=n_kt,
                                      nk=len(kts), ob=ob, exp_evA=exp_evA,
                                      mask_evV=mask_evV, ii=ii):
                                if mask_evV is not None:
                                    p.wait("tensor", p.evV, mask_evV)
                                else:
                                    p.wait("tensor", p.evA, exp_evA)
                                p.ins("tensor", lambda e: e.matmul(
                                    pb[ob][0:65, live0:512],
                                    vb[:, kt * 520 + h * 65: kt * 520 + (h + 1) * 65],
                                    ptb[:, ps * 512 + live0: (ps + 1) * 512],
                                    start=(n_kt == 0), stop=(n_kt == nk - 1)),
                                    inc=(p.mm, 1))
                                av_last[0] = p.mm.n
                                avrot[ii] = p.mm.n
                            av_q[0] = mk_av
                        emit_av()
                        flush_tail()
                        p.wait("vector", p.mm, av_last[0])
                        p.ins("vector", lambda e, ob=ob: e.reciprocal(
                            zst[64:65, :], pb[ob][64:65, :]))
                        pending[0] = (h, ob, p.evV.n)
                    flush_tail()
                    MSG_READY = p.evV.n
                    wo_mm = [0, 0]
                    for d in range(DC):
                        bank = 0 + (d % 2)
                        slot = d % 2
                        if d >= 2:
                            p.wait("sync", p.mm, wo_mm[slot])
                        tkwo = self.dqs.dma(lambda e, d=d, H=H, slot=slot: e.dma_start(
                            out=wob[:, slot * 1024:(slot + 1) * 1024],
                            in_=ap["s_Wo"][d, :, H * 1024:(H + 1) * 1024]))
                        p.wait("tensor", p.evV, MSG_READY)
                        self.dqs.wait("tensor", tkwo)
                        if d >= 2:
                            p.wait("tensor", p.evV)
                        for ci in range(8):
                            p.ins("tensor", lambda e, ci=ci, d=d, bank=bank, slot=slot: e.matmul(
                                pb[bank][:, :],
                                wob[:, slot * 1024 + ci * 128: slot * 1024 + (ci + 1) * 128],
                                msgb[:, ci * 512:(ci + 1) * 512],
                                start=(ci == 0), stop=(ci == 7)),
                                inc=(p.mm, 1) if ci == 7 else None)
                        wo_mm[slot] = p.mm.n
                        p.wait("vector", p.mm)
                        p.ins("vector", lambda e, d=d, col0=col0, bank=bank: e.tensor_add(
                            self.xres[:, d * T + col0: d * T + col0 + 512],
                            self.xres[:, d * T + col0: d * T + col0 + 512],
                            pb[bank][:, :]))
                        kfree[bank] = (p.evV, p.evV.n)
        p.fence()

    def emit_store(self):
        p = self.p
        p.fence()
        for c in range(DC):
            self.dqs.dma(lambda e, c=c: e.dma_start(
                out=self.yT[c, :, :], in_=self.xres[:, c * T:(c + 1) * T]))
        self.dqs.drain("sync")
        self.dqg.drain("sync")

    def realize(self):
        p = self.p
        with self.nc.Block() as block:
            @block.tensor
            def _(e):
                for fn in p.streams["tensor"]:
                    fn(e)

            @block.vector
            def _(e):
                for fn in p.streams["vector"]:
                    fn(e)

            @block.scalar
            def _(e):
                for fn in p.streams["scalar"]:
                    fn(e)

            @block.gpsimd
            def _(e):
                for fn in p.streams["gpsimd"]:
                    fn(e)

            @block.sync
            def _(e):
                for fn in p.streams["sync"]:
                    fn(e)
        return self.nc


def build_nc(stages=STAGES):
    b = Build(stages)
    b.emit_init()
    b.emit_prep()
    for s in sorted(stages):
        if s in (1, 3):
            b.emit_l0ffn()
        elif s == 2:
            b.emit_l0attn()
        elif s in (4, 6):
            b.emit_l1ffn()
        elif s == 5:
            b.emit_l1attn()
        elif s == 7:
            b.emit_l0attn_inner()
    b.emit_store()
    return b.realize()



def kernel(**inputs):
    d = inputs
    f32 = np.float32

    shared = {}
    shared["wup_p"] = pack_w_outmajor(np.asarray(d["L0_ffn_w_up"], f32))
    shared["wdn_p"] = pack_w_outmajor(np.asarray(d["L0_ffn_w_down"], f32))
    shared["w1_p"] = pack_w_outmajor(np.asarray(d["L0_attn_w1"], f32))
    shared["w2_p"] = pack_w_outmajor(np.asarray(d["L0_attn_w2"], f32))
    shared["wq_p"] = pack_attn_outmajor(np.asarray(d["L0_attn_wq"], f32))
    shared["wk_p"] = pack_attn_outmajor(np.asarray(d["L0_attn_wk"], f32))
    shared["wv_p"] = pack_attn_outmajor(np.asarray(d["L0_attn_wv"], f32))
    shared["Wq_p"] = pack_attn_inmajor(np.asarray(d["L1_attn_Wq"], f32))
    shared["Wk_p"] = pack_attn_inmajor(np.asarray(d["L1_attn_Wk"], f32))
    shared["Wv_p"] = pack_attn_inmajor(np.asarray(d["L1_attn_Wv"], f32))
    wo = np.asarray(d["L1_attn_Wo"], f32)
    wo_e = np.zeros((2048, 1024), f32)
    wo_e.reshape(16, 128, 1024)[:, 0:64, :] = wo.reshape(16, 64, 1024)
    shared["Wo_p"] = pack_w_inmajor(wo_e)
    shared["fc1_p"] = pack_w_inmajor(np.asarray(d["L1_ffn_fc1_q"], np.int32))
    shared["fc2_p"] = pack_w_inmajor(np.asarray(d["L1_ffn_fc2_q"], np.int32))
    fc1s = np.ascontiguousarray(
        np.asarray(d["L1_ffn_fc1_scales"], f32).reshape(8, 32, 128)
        .transpose(1, 0, 2).reshape(32, 1024))
    fc2s = np.ascontiguousarray(
        np.asarray(d["L1_ffn_fc2_scales"], f32).reshape(32, 8, 128)
        .transpose(1, 0, 2).reshape(8, 4096))
    shared["fc1s_p"] = np.ascontiguousarray(
        np.broadcast_to(fc1s[:, None, :], (32, 128, 1024)))
    shared["fc2s_p"] = np.ascontiguousarray(
        np.broadcast_to(fc2s[:, None, :], (8, 128, 4096)))

    vecs = np.zeros((128, 160), f32)
    vecs[:, 0:8] = pack_vec(np.asarray(d["L0_attn_g1"], f32))
    vecs[:, 8:16] = pack_vec(np.asarray(d["L0_attn_be1"], f32))
    vecs[:, 16:24] = pack_vec(np.asarray(d["L0_attn_g2"], f32))
    vecs[:, 24:32] = pack_vec(np.asarray(d["L0_attn_be2"], f32))
    vecs[:, 32:64] = pack_vec(np.asarray(d["L0_attn_b1"], f32))
    vecs[:, 64:72] = pack_vec(np.asarray(d["L0_attn_b2"], f32))
    vecs[:, 72:80] = pack_vec(np.asarray(d["L1_ffn_ln_gamma"], f32))
    vecs[:, 80:88] = pack_vec(np.asarray(d["L1_ffn_ln_beta"], f32))
    vecs[:, 88:120] = pack_vec(np.asarray(d["L1_ffn_fc1_bias"], f32))
    vecs[:, 120:128] = pack_vec(np.asarray(d["L1_ffn_fc2_bias"], f32))
    vecs[:, 128:136] = pack_vec(np.asarray(d["L1_attn_bq"], f32))
    vecs[:, 136:144] = pack_vec(np.asarray(d["L1_attn_bk"], f32))
    shared["vecs"] = vecs

    rowsin = np.zeros((1, 1024), f32)
    rowsin[0, 0:1024] = np.asarray(d["L1_attn_bv"], f32)
    shared["rowsin"] = rowsin

    cp = np.zeros(16, f32)
    cp[0], cp[1], cp[2], cp[3], cp[4] = 1e-5, 1e-8, 1e-6, EXP_SHIFT, 0.125
    cp[5], cp[6], cp[7], cp[8], cp[9] = 0.25, 0.5, 0.0, 1.0, -128.0
    cp[10], cp[11], cp[12], cp[13] = 1.0 / D, -1.0 / D, 1.0 / FF, -1.0 / FF
    shared["cpool"] = np.tile(cp[None, :], (128, 1))
    shared["identf"] = np.eye(128, dtype=f32)
    shared["identb"] = np.eye(128, dtype=ml_dtypes.bfloat16)
    shared["onesf"] = np.ones((128, 128), f32)
    shared["onesb"] = np.ones((128, 128), ml_dtypes.bfloat16)
    shared["maskb"] = np.triu(np.ones((128, 128), ml_dtypes.bfloat16))

    x = np.asarray(d["L0_attn_x"], f32)
    in_maps = []
    for c in range(NCORES):
        xc = x[SEQ_PER_CORE * c: SEQ_PER_CORE * (c + 1)].reshape(T, D)
        xT = np.ascontiguousarray(xc.reshape(T, DC, 128).transpose(1, 2, 0))
        m = dict(shared)
        m["xT"] = xT
        in_maps.append(m)

    nc = build_nc(tuple(STAGES))
    res = run_bass_kernel_spmd(nc, in_maps, core_ids=list(range(NCORES)))
    out = np.empty((B, S, D), f32)
    for c in range(NCORES):
        yt = np.asarray(res.results[c]["yT"])
        xc = yt.transpose(2, 0, 1).reshape(T, D)
        out[SEQ_PER_CORE * c: SEQ_PER_CORE * (c + 1)] = xc.reshape(SEQ_PER_CORE, S, D)
    return out


# version 13 (final)
# speedup vs baseline: 1.0275x; 1.0275x over previous
import numpy as np
import ml_dtypes

import concourse.bass as bass
import concourse.mybir as mybir
from concourse.bass_utils import run_bass_kernel_spmd

F32 = mybir.dt.float32
BF16 = mybir.dt.bfloat16
I32 = mybir.dt.int32
ALU = mybir.AluOpType
ACTF = mybir.ActivationFunctionType

NCORES = 8
B, S, D, FF = 16, 2048, 1024, 4096
SEQ_PER_CORE = B // NCORES
T = SEQ_PER_CORE * S
DC = D // 128
FC = FF // 128
EXP_SHIFT = -10.0

STAGES = (1, 2, 3, 4, 5, 6)
DBG_NT_LIMIT = []
DBG_OM = []

ENGS = ("tensor", "vector", "scalar", "gpsimd", "sync")


def pack_w_inmajor(w):
    K, N = w.shape
    return np.ascontiguousarray(
        w.reshape(K // 128, 128, N // 128, 128).transpose(2, 1, 0, 3).reshape(N // 128, 128, K))


def pack_w_outmajor(w):
    N, K = w.shape
    return np.ascontiguousarray(
        w.reshape(N // 128, 128, K // 128, 128).transpose(0, 3, 2, 1).reshape(N // 128, 128, K))


def pack_vec(v):
    n = v.shape[0] // 128
    return np.ascontiguousarray(v.reshape(n, 128).T)


def pack_attn_outmajor(w):
    N, K = w.shape
    return np.ascontiguousarray(w.reshape(N, K // 128, 128).transpose(1, 2, 0))


def pack_attn_inmajor(w):
    K, N = w.shape
    return np.ascontiguousarray(w.reshape(K // 128, 128, N))



class Sem:
    def __init__(self, nc, name):
        self.h = nc.alloc_semaphore(name)
        self.n = 0


class Prog:
    def __init__(self, nc):
        self.nc = nc
        self.streams = {e: [] for e in ENGS}
        self.mm = Sem(nc, "mm")
        self.evA = Sem(nc, "evA")
        self.evV = Sem(nc, "evV")
        self.sems = [self.mm, self.evA, self.evV]

    def emit(self, eng, fn):
        self.streams[eng].append(fn)

    def wait(self, eng, sem, val=None):
        v = sem.n if val is None else val
        if v <= 0:
            return
        self.emit(eng, lambda e, s=sem.h, v=v: e.wait_ge(s, v))

    def ins(self, eng, build, inc=None):
        if eng == "vector":
            own = self.evV
        elif eng == "scalar":
            own = self.evA
        else:
            own = None
        if own is not None:
            if own.n > 0:
                self.emit(eng, lambda e, s=own.h, v=own.n: e.wait_ge(s, v))
            if inc is None:
                inc = (own, 1)
            else:
                assert inc[0] is own, "vector/scalar ops must inc their own sem"
        if inc is not None:
            inc[0].n += inc[1]
            h, k = inc[0].h, inc[1]
            self.emit(eng, lambda e, build=build, h=h, k=k: build(e).then_inc(h, k))
        else:
            self.emit(eng, lambda e, build=build: build(e))

    def fence(self, skip=()):
        snap = [(s.h, s.n) for s in self.sems if s.n > 0]
        for dq in getattr(self, "dqs", []):
            if dq in skip:
                continue
            snap += [(s.h, s.n) for s in dq.sems if s.n > 0]
        for eng in ENGS:
            for h, v in snap:
                self.emit(eng, lambda e, h=h, v=v: e.wait_ge(h, v))


class Dq:
    def __init__(self, p, eng, name, depth=4):
        self.p, self.eng = p, eng
        self.sems = [Sem(p.nc, f"{name}{i}") for i in range(depth)]
        self.depth = depth
        self.i = 0
        p.dqs = getattr(p, "dqs", [])
        p.dqs.append(self)

    def dma(self, build):
        s = self.sems[self.i % self.depth]
        self.p.wait(self.eng, s, s.n)
        self.p.ins(self.eng, build, inc=(s, 16))
        self.i += 1
        return (s, s.n)

    def wait(self, eng, tok):
        self.p.wait(eng, tok[0], tok[1])

    def drain(self, eng):
        for s in self.sems:
            self.p.wait(eng, s, s.n)


class Arena:
    def __init__(self, nc, base, size):
        self.nc = nc
        self.base = base
        self.size = size
        self.count = 0

    def reserve_tail(self, specs):
        self.count += 1
        need = 0
        for name, shape, dtype in specs:
            need = ((need + 63) & ~63) + shape[1] * mybir.dt.size(dtype)
        need = (need + 63) & ~63
        self.size -= need
        out = {}
        off = self.size
        for name, shape, dtype in specs:
            off = (off + 63) & ~63
            out[name] = self.nc.alloc_sbuf_tensor_at(
                f"ar{self.count}_{name}", shape, dtype, offset=self.base + off)
            off += shape[1] * mybir.dt.size(dtype)
        out["_need"] = need
        return out

    def release_tail(self, need):
        self.size += need

    def layout(self, specs):
        self.count += 1
        out = {}
        off = 0
        for name, shape, dtype in specs:
            nbytes = shape[1] * mybir.dt.size(dtype)
            off = (off + 63) & ~63
            t = self.nc.alloc_sbuf_tensor_at(
                f"ar{self.count}_{name}", shape, dtype, offset=self.base + off)
            out[name] = t
            off += nbytes
        assert off <= self.size, f"arena overflow: {off} > {self.size}"
        return out



class Build:
    def __init__(self, stages):
        self.stages = stages
        nc = self.nc = bass.Bass("TRN2", target_bir_lowering=False)

        ap = self.ap = {}

        def dram_in(name, shape, dtype=F32):
            ap[name] = nc.dram_tensor(name, list(shape), dtype, kind="ExternalInput").ap()

        dram_in("xT", (DC, 128, T))
        dram_in("wup_p", (FC, 128, D))
        dram_in("wdn_p", (DC, 128, FF))
        dram_in("w1_p", (FC, 128, D))
        dram_in("w2_p", (DC, 128, FF))
        dram_in("wq_p", (DC, 128, D))
        dram_in("wk_p", (DC, 128, D))
        dram_in("wv_p", (DC, 128, D))
        dram_in("Wq_p", (DC, 128, D))
        dram_in("Wk_p", (DC, 128, D))
        dram_in("Wv_p", (DC, 128, D))
        dram_in("Wo_p", (DC, 128, 2 * D))
        dram_in("fc1_p", (FC, 128, D), I32)
        dram_in("fc2_p", (DC, 128, FF), I32)
        dram_in("fc1s_p", (FC, 128, D))
        dram_in("fc2s_p", (DC, 128, FF))
        dram_in("vecs", (128, 160))
        dram_in("rowsin", (1, 1024))
        dram_in("cpool", (128, 16))
        dram_in("identf", (128, 128))
        dram_in("identb", (128, 128), BF16)
        dram_in("onesf", (128, 128))
        dram_in("onesb", (128, 128), BF16)
        dram_in("maskb", (128, 128), BF16)
        self.yT = nc.dram_tensor("yT", [DC, 128, T], F32, kind="ExternalOutput").ap()

        def scratch(name, shape, dtype):
            ap[name] = nc.dram_tensor(name, list(shape), dtype, kind="Internal").ap()

        scratch("s_wup", (FC, 128, D), BF16)
        scratch("s_wdn", (DC, 128, FF), BF16)
        scratch("s_w1", (FC, 128, D), BF16)
        scratch("s_w2", (DC, 128, FF), BF16)
        scratch("s_wq", (DC, 128, D), BF16)
        scratch("s_wk", (DC, 128, D), BF16)
        scratch("s_wv", (DC, 128, D), BF16)
        scratch("s_Wq", (DC, 128, D), BF16)
        scratch("s_Wk", (DC, 128, D), BF16)
        scratch("s_Wv", (DC, 128, D), BF16)
        scratch("s_Wo", (DC, 128, 2 * D), BF16)
        scratch("s_fc1", (FC, 128, D), BF16)
        scratch("s_fc2", (DC, 128, FF), BF16)
        scratch("s_x2", (DC, 128, T), BF16)
        scratch("s_s", (DC, 128, T), F32)
        scratch("s_dmsg", (DC, 128, T), F32)

        self.xres = nc.alloc_sbuf_tensor("xres", [128, DC * T], F32)
        self.cpool = nc.alloc_sbuf_tensor("cpoolT", [128, 16], F32)
        self.onesf = nc.alloc_sbuf_tensor("onesfT", [128, 128], F32)
        self.onesb = nc.alloc_sbuf_tensor("onesbT", [128, 128], BF16)
        self.identf = nc.alloc_sbuf_tensor("identfT", [128, 128], F32)
        self.identb = nc.alloc_sbuf_tensor("identbT", [128, 128], BF16)
        self.maskb = nc.alloc_sbuf_tensor("maskbT", [128, 128], BF16)
        self.vecs = nc.alloc_sbuf_tensor("vecsT", [128, 160], F32)
        self.rows = nc.alloc_sbuf_tensor("rowsT", [128, 1024], F32)
        self.stat = nc.alloc_sbuf_tensor("statT", [128, 64], F32)

        rem = nc.sbuf_bytes_remaining
        self.ARENA_BYTES = rem - 32
        r = nc.bump_sbuf(self.ARENA_BYTES)
        base = r[0] if isinstance(r, tuple) else r
        self.arena = Arena(nc, base, self.ARENA_BYTES)

        self.pb = [nc.alloc_psum_tensor(f"pb{i}", [128, 512], F32) for i in range(6)]
        self.bb = [nc.alloc_psum_tensor(f"bbk{i}", [128, 1024], BF16) for i in range(2)]

        self.p = Prog(nc)
        self.dqs = Dq(self.p, "sync", "qs", depth=6)
        self.dqg = Dq(self.p, "gpsimd", "qg", depth=4)

    C_LN_EPS, C_RMS_EPS, C_ATT_EPS, C_EXPB, C_QK, C_QUART, C_HALF, C_ZERO, \
        C_ONE, C_NEG128, C_INV_D, C_NINV_D, C_INV_FF, C_NINV_FF = range(14)
    V_G1, V_BE1, V_G2, V_BE2 = 0, 8, 16, 24
    V_B1, V_B2 = 32, 64
    V_LNG, V_LNB = 72, 80
    V_FC1B, V_QFC2B = 88, 120
    V_BQ, V_BK = 128, 136

    def pump_side(self, n):
        while n > 0 and self.side_work:
            try:
                next(self.side_work[0])
                n -= 1
            except StopIteration:
                self.side_work.pop(0)

    def drain_side(self):
        while self.side_work:
            try:
                next(self.side_work[0])
            except StopIteration:
                self.side_work.pop(0)
        if self.side_res:
            self.arena.release_tail(self.side_res)
            self.side_res = None

    def cp(self, i):
        return self.cpool[:, i:i + 1]

    def ST(self, i):
        return self.stat[:, i:i + 1]

    def emit_init(self):
        p, ap = self.p, self.ap
        ctoks = []
        for (dst, src) in ((self.cpool, "cpool"), (self.onesf, "onesf"),
                           (self.onesb, "onesb"), (self.identf, "identf"),
                           (self.identb, "identb"), (self.maskb, "maskb"),
                           (self.vecs, "vecs")):
            ctoks.append(self.dqs.dma(
                lambda e, d=dst, s=src: e.dma_start(out=d[:, :], in_=ap[s][:, :])))
        ctoks.append(self.dqs.dma(
            lambda e: e.dma_start(out=self.rows[0:1, :], in_=ap["rowsin"][:, :])))
        self.tok_x = []
        for c in range(DC):
            self.tok_x.append(self.dqs.dma(lambda e, c=c: e.dma_start(
                out=self.xres[:, c * T:(c + 1) * T], in_=ap["xT"][c, :, :])))
        for eng in ENGS:
            for tk in ctoks:
                self.dqs.wait(eng, tk)
        p.ins("vector", lambda e: e.tensor_scalar(
            self.vecs[:, self.V_QFC2B:self.V_QFC2B + 8],
            self.vecs[:, self.V_QFC2B:self.V_QFC2B + 8], 0.25, None, ALU.mult),
            inc=(p.evV, 1))

    def emit_prep(self):
        p, ap = self.p, self.ap
        stages = self.stages
        casts = []
        if 2 in stages or 7 in stages:
            casts += [("wq_p", "s_wq"), ("wk_p", "s_wk"), ("wv_p", "s_wv"),
                      ("w1_p", "s_w1"), ("w2_p", "s_w2")]
        if 5 in stages:
            casts += [("Wq_p", "s_Wq"), ("Wk_p", "s_Wk"), ("Wv_p", "s_Wv"),
                      ("Wo_p", "s_Wo")]
        self.cast_tok = {}
        for (src, dst) in casts:
            self.cast_tok[dst] = self.dqg.dma(lambda e, s=src, d=dst: e.dma_start(
                out=ap[d][:, :, :], in_=ap[s][:, :, :]))
        if 1 in stages or 3 in stages:
            self._tern(("wup_p", "s_wup", FC, D))
            p.fence()
            self._tern(("wdn_p", "s_wdn", DC, FF))
            p.fence()
        self.side_work = []
        self.side_res = None
        p.fence(skip=(self.dqg,))

    def _tern(self, which):
        src, dst, OC, IC = which
        HIC = min(IC, 1024)
        nh = IC // HIC
        p, ap, pb = self.p, self.ap, self.pb
        nel = OC * 128 * IC
        NB = 4
        ar = self.arena.layout([
            ("ti", [128, NB * HIC], F32),
            ("tsg", [128, 2 * HIC], F32),
            ("tab", [128, 2 * HIC], F32),
            ("tmk", [128, 2 * HIC], F32),
            ("to", [128, NB * HIC], BF16),
            ("acc", [128, 64], F32),
        ])
        ti, tsg, tab, tmk, to, acc = (ar[k] for k in ("ti", "tsg", "tab", "tmk", "to", "acc"))
        tiles = [(oc, hf) for oc in range(OC) for hf in range(nh)]
        n = len(tiles)
        absA = []
        for i, (oc, hf) in enumerate(tiles):
            buf = i % NB
            if i >= NB:
                p.wait("gpsimd", p.evA, absA[i - NB])
            tk = self.dqg.dma(lambda e, oc=oc, hf=hf, buf=buf: e.dma_start(
                out=ti[:, buf * HIC:(buf + 1) * HIC],
                in_=ap[src][oc, :, hf * HIC:(hf + 1) * HIC]))
            self.dqg.wait("scalar", tk)
            p.ins("scalar", lambda e, buf=buf, i=i: e.activation(
                tab[:, 0:HIC], ti[:, buf * HIC:(buf + 1) * HIC], ACTF.Abs,
                accum_out=acc[:, i:i + 1]))
            absA.append(p.evA.n)
        p.wait("tensor", p.evA, p.evA.n)
        p.ins("tensor", lambda e, n=n: e.matmul(
            pb[0][0:1, 0:n], self.onesf[:, 0:1], acc[:, 0:n],
            start=True, stop=True), inc=(p.mm, 1))
        p.wait("vector", p.mm, p.mm.n)
        p.ins("vector", lambda e, n=n: e.reduce_sum(
            acc[0:1, 0:1], pb[0][0:1, 0:n], mybir.AxisListType.X))
        p.ins("vector", lambda e: e.tensor_scalar(acc[0:1, 0:1], acc[0:1, 0:1],
                                                  0.5 / nel, None, ALU.mult))
        p.wait("tensor", p.evV, p.evV.n)
        p.ins("tensor", lambda e: e.matmul(pb[0][:, 1:2], self.onesf[0:1, :], acc[0:1, 0:1],
                                           start=True, stop=True), inc=(p.mm, 1))
        p.wait("vector", p.mm, p.mm.n)
        p.ins("vector", lambda e: e.tensor_copy(acc[:, 1:2], pb[0][:, 1:2]))
        p.wait("gpsimd", p.evA, p.evA.n)
        otoks = []
        inA = []
        mulV = []
        for i, (oc, hf) in enumerate(tiles):
            buf = i % NB
            pr = i % 2
            if i >= NB:
                p.wait("gpsimd", p.evA, inA[i - NB])
                p.wait("gpsimd", p.evV, mulV[i - 2])
            tk = self.dqg.dma(lambda e, oc=oc, hf=hf, buf=buf: e.dma_start(
                out=ti[:, buf * HIC:(buf + 1) * HIC],
                in_=ap[src][oc, :, hf * HIC:(hf + 1) * HIC]))
            self.dqg.wait("scalar", tk)
            if i >= 2:
                p.wait("scalar", p.evV, mulV[i - 2])
            p.ins("scalar", lambda e, buf=buf, pr=pr: e.activation(
                tsg[:, pr * HIC:(pr + 1) * HIC], ti[:, buf * HIC:(buf + 1) * HIC],
                ACTF.Sign))
            p.ins("scalar", lambda e, buf=buf, pr=pr: e.activation(
                tab[:, pr * HIC:(pr + 1) * HIC], ti[:, buf * HIC:(buf + 1) * HIC],
                ACTF.Abs))
            inA.append(p.evA.n)
            p.wait("vector", p.evA, p.evA.n)
            if i >= NB:
                self.dqs.wait("vector", otoks[i - NB])
            p.ins("vector", lambda e, pr=pr: e.tensor_scalar(
                tmk[:, pr * HIC:(pr + 1) * HIC], tab[:, pr * HIC:(pr + 1) * HIC],
                acc[:, 1:2], None, ALU.is_gt))
            p.ins("vector", lambda e, buf=buf, pr=pr: e.tensor_mul(
                to[:, buf * HIC:(buf + 1) * HIC], tsg[:, pr * HIC:(pr + 1) * HIC],
                tmk[:, pr * HIC:(pr + 1) * HIC]))
            mulV.append(p.evV.n)
            p.wait("sync", p.evV, p.evV.n)
            otoks.append(self.dqs.dma(lambda e, oc=oc, hf=hf, buf=buf: e.dma_start(
                out=ap[dst][oc, :, hf * HIC:(hf + 1) * HIC],
                in_=to[:, buf * HIC:(buf + 1) * HIC])))

    def _deq(self, which, ar):
        qsrc, ssrc, dst, OC, IC, HIC = which
        p, ap, pb, rows = self.p, self.ap, self.pb, self.rows
        NB = 3
        qi, qf, qs, qo = ar["qi"], ar["qf"], ar["qs"], ar["qo"]
        nh = IC // HIC
        tiles = [(oc, hf) for oc in range(OC) for hf in range(nh)]
        n = len(tiles)
        otoks = {}
        cpyA = {}
        mulV = {}
        qtok = {}
        stok = {}

        def issue(i):
            oc, hf = tiles[i]
            buf = i % NB
            pr = i % 2
            if i >= NB:
                p.wait("gpsimd", p.evA, cpyA[i - NB])
            qtok[i] = self.dqg.dma(lambda e, oc=oc, hf=hf, buf=buf: e.dma_start(
                out=qi[:, buf * HIC:(buf + 1) * HIC],
                in_=ap[qsrc][oc, :, hf * HIC:(hf + 1) * HIC]))
            if i >= 2:
                p.wait("gpsimd", p.evV, mulV[i - 2])
            stok[i] = self.dqg.dma(lambda e, oc=oc, hf=hf, pr=pr: e.dma_start(
                out=qs[:, pr * HIC:(pr + 1) * HIC],
                in_=ap[ssrc][oc, :, hf * HIC:(hf + 1) * HIC]))

        def compute(i):
            oc, hf = tiles[i]
            buf = i % NB
            pr = i % 2
            self.dqg.wait("scalar", qtok[i])
            if i >= 2:
                p.wait("scalar", p.evV, mulV[i - 2])
            p.ins("scalar", lambda e, buf=buf, pr=pr: e.activation(
                qf[:, pr * HIC:(pr + 1) * HIC], qi[:, buf * HIC:(buf + 1) * HIC],
                ACTF.Copy, bias=-128.0))
            cpyA[i] = p.evA.n
            self.dqg.wait("vector", stok[i])
            p.wait("vector", p.evA, p.evA.n)
            if i >= NB:
                self.dqg.wait("vector", otoks[i - NB])
            p.ins("vector", lambda e, buf=buf, pr=pr: e.tensor_mul(
                qo[:, buf * HIC:(buf + 1) * HIC],
                qf[:, pr * HIC:(pr + 1) * HIC],
                qs[:, pr * HIC:(pr + 1) * HIC]))
            mulV[i] = p.evV.n
            p.wait("gpsimd", p.evV, p.evV.n)
            otoks[i] = self.dqg.dma(lambda e, oc=oc, hf=hf, buf=buf: e.dma_start(
                out=ap[dst][oc, :, hf * HIC:(hf + 1) * HIC],
                in_=qo[:, buf * HIC:(buf + 1) * HIC]))

        self.dqg.drain("vector")
        self.dqg.drain("scalar")
        self.dqg.drain("gpsimd")
        issue(0)
        if n > 1:
            issue(1)
        for i in range(n):
            compute(i)
            if i + 2 < n:
                issue(i + 2)
            yield

    def emit_ln_tile(self, cfg):
        p, pb, bbk = self.p, self.pb, self.bb
        kind = cfg["kind"]
        src = cfg["src"]
        NJ = cfg["nj"]
        trash, lnT = cfg["trash"], cfg["lnT"]
        lnTf = cfg.get("lnTf")
        bf = cfg["bank_free"]
        fbp = cfg.get("fb_pair", (0, 1))
        sbase = cfg.get("stat_base", 0)
        bbctr = cfg.get("bbctr")
        ST_ = self.ST
        ST = lambda i: ST_(sbase + i)

        def bwait(eng, key):
            s, v = bf[key]
            p.wait(eng, s, v)

        for j in range(NJ):
            yield ("fwd", j)
            fb = [pb[fbp[0]], pb[fbp[1]]]
            bwait("tensor", fbp[0])
            bwait("tensor", fbp[1])
            if j == 0:
                for s_, v_ in cfg.get("fwd_extra_wait", []):
                    p.wait("tensor", s_, v_)
            for c in range(DC):
                p.ins("tensor", lambda e, c=c, j=j: e.transpose(
                    fb[c // 4][:, (c % 4) * 128:(c % 4 + 1) * 128], src(c, j),
                    self.identf[:, :]), inc=(p.mm, 1) if c == DC - 1 else None)
            fwd_mm = p.mm.n
            p.wait("scalar", p.mm, fwd_mm)
            for h2 in range(2):
                p.ins("scalar", lambda e, h2=h2: e.activation(
                    trash[:, 0:512], fb[h2][:, :], ACTF.Copy, accum_out=ST(0 + h2)))
                p.ins("scalar", lambda e, h2=h2: e.activation(
                    trash[:, 0:512], fb[h2][:, :], ACTF.Square, accum_out=ST(2 + h2)))
            stats_evA = p.evA.n
            p.wait("vector", p.evA, stats_evA)
            p.ins("vector", lambda e: e.tensor_add(ST(4), ST(0), ST(1)))
            p.ins("vector", lambda e: e.tensor_add(ST(5), ST(2), ST(3)))
            p.ins("vector", lambda e: e.tensor_scalar(ST(6), ST(4), -1.0 / D, None, ALU.mult))
            p.ins("vector", lambda e: e.tensor_scalar(ST(8), ST(5), 1.0 / D, None, ALU.mult))
            p.ins("vector", lambda e: e.tensor_mul(ST(7), ST(6), ST(6)))
            p.ins("vector", lambda e: e.tensor_sub(ST(7), ST(8), ST(7)))
            p.wait("scalar", p.evV, p.evV.n)
            p.ins("scalar", lambda e: e.activation(ST(8), ST(7), ACTF.Sqrt,
                                                   bias=self.cp(self.C_LN_EPS)))
            p.wait("vector", p.evA, p.evA.n)
            p.ins("vector", lambda e: e.reciprocal(ST(9), ST(8)))
            if kind == "l0ffn":
                p.wait("scalar", p.evV, p.evV.n)
                p.ins("scalar", lambda e: e.activation(ST(8), ST(7), ACTF.Sqrt,
                                                       bias=self.cp(self.C_ZERO)))
                p.wait("vector", p.evA, p.evA.n)
                p.ins("vector", lambda e: e.tensor_mul(ST(12), ST(8), ST(9)))
                p.ins("vector", lambda e: e.tensor_scalar(ST(12), ST(12), 1e-8, None, ALU.add))
                p.ins("vector", lambda e: e.reciprocal(ST(12), ST(12)))
                p.ins("vector", lambda e: e.tensor_mul(ST(10), ST(9), ST(12)))
                ascale = 10
            elif kind == "l1ffn":
                p.ins("vector", lambda e: e.tensor_mul(ST(12), ST(9), ST(9)))
                p.ins("vector", lambda e: e.tensor_mul(ST(12), ST(12), ST(7)))
                p.wait("scalar", p.evV, p.evV.n)
                p.ins("scalar", lambda e: e.activation(ST(12), ST(12), ACTF.Sqrt,
                                                       bias=self.cp(self.C_LN_EPS)))
                p.wait("vector", p.evA, p.evA.n)
                p.ins("vector", lambda e: e.reciprocal(ST(11), ST(12)))
                ascale = 9
            else:
                ascale = 9
            if kind in ("l1ffn", "l0attn"):
                for h2 in range(2):
                    p.ins("vector", lambda e, h2=h2: e.tensor_scalar(
                        lnTf[:, h2 * 512:(h2 + 1) * 512], fb[h2][:, :], ST(6), ST(9),
                        ALU.add, ALU.mult))
                if kind == "l1ffn":
                    for h2 in range(2):
                        p.ins("vector", lambda e, h2=h2: e.tensor_scalar(
                            lnT[:, h2 * 512:(h2 + 1) * 512],
                            lnTf[:, h2 * 512:(h2 + 1) * 512], ST(11), None, ALU.mult))
                else:
                    for h2 in range(2):
                        p.ins("vector", lambda e, h2=h2: e.tensor_copy(
                            lnT[:, h2 * 512:(h2 + 1) * 512],
                            lnTf[:, h2 * 512:(h2 + 1) * 512]))
            else:
                for h2 in range(2):
                    p.ins("vector", lambda e, h2=h2, a=ascale: e.tensor_scalar(
                        lnT[:, h2 * 512:(h2 + 1) * 512], fb[h2][:, :], ST(6), ST(a),
                        ALU.add, ALU.mult))
            bf[fbp[0]] = bf[fbp[1]] = (p.evV, p.evV.n)
            apply_evV = p.evV.n

            yield ("back", j)
            if bbctr is not None:
                bbi = bbctr[0] % 2
                bbctr[0] += 1
            else:
                bbi = j % 2
            bbkey = f"bb{bbi}"
            p.wait("tensor", p.evV, apply_evV)
            bwait("tensor", bbkey)
            for c in range(DC):
                p.ins("tensor", lambda e, c=c, bbi=bbi: e.transpose(
                    bbk[bbi][:, c * 128:(c + 1) * 128], lnT[:, c * 128:(c + 1) * 128],
                    self.identb[:, :]), inc=(p.mm, 1) if c == DC - 1 else None)
            if kind in ("l1ffn", "l0attn"):
                bwait("tensor", 4)
                bwait("tensor", 5)
                for c in range(DC):
                    p.ins("tensor", lambda e, c=c: e.transpose(
                        pb[4 + c // 4][:, (c % 4) * 128:(c % 4 + 1) * 128],
                        lnTf[:, c * 128:(c + 1) * 128], self.identf[:, :]),
                        inc=(p.mm, 1) if c == DC - 1 else None)
            back_mm = p.mm.n
            p.wait("vector", p.mm, back_mm)
            if j == 0:
                for s_, v_ in cfg.get("x2_write_wait", []):
                    p.wait("vector", s_, v_)
            if kind in ("lngb", "l0attn", "l1ffn"):
                gam, bet = cfg["gam"], cfg["bet"]
                for c in range(DC):
                    p.ins("vector", lambda e, c=c, j=j, bbi=bbi: e.tensor_scalar(
                        cfg["x2_out"](c, j), bbk[bbi][:, c * 128:(c + 1) * 128],
                        self.vecs[:, gam + c:gam + c + 1], self.vecs[:, bet + c:bet + c + 1],
                        ALU.mult, ALU.add))
            else:
                p.ins("vector", lambda e, j=j, bbi=bbi: e.tensor_copy(
                    cfg["x2_out_blk"](j),
                    bbk[bbi][:, :].rearrange("q (c w) -> q c w", c=DC)))
            bf[bbkey] = (p.evV, p.evV.n)
            if kind == "l1ffn":
                for c in range(DC):
                    p.ins("vector", lambda e, c=c, j=j: e.scalar_tensor_tensor(
                        cfg["xslice"](c, j),
                        pb[4 + c // 4][:, (c % 4) * 128:(c % 4 + 1) * 128],
                        0.5, cfg["xslice"](c, j), ALU.mult, ALU.add))
                bf[4] = bf[5] = (p.evV, p.evV.n)
            elif kind == "l0attn":
                for c in range(DC):
                    p.ins("vector", lambda e, c=c, j=j: e.tensor_copy(
                        cfg["s_out"](c, j),
                        pb[4 + c // 4][:, (c % 4) * 128:(c % 4 + 1) * 128]))
                bf[4] = bf[5] = (p.evV, p.evV.n)

    def drive_ln(self, cfg):
        for _ in self.emit_ln_tile(cfg):
            pass


    def emit_ffn(self, cfg):
        p, pb, ap = self.p, self.pb, self.ap
        xres, vecs, rows = self.xres, self.vecs, self.rows
        TT = cfg.get("tt", 512)
        NJ = TT // 128
        NT = cfg.get("nt_limit", T // TT)
        is_scratch = cfg["src"] != "xres"
        h_rms = bool(cfg.get("h_rms"))
        XW = DC * TT
        specs = [
            ("x2t", [128, 2 * XW], BF16),
            ("ht", [128, FC * TT], BF16),
            ("urng", [128, 2 * 2048], BF16),
            ("drng", [128, 3 * 1024], BF16),
            ("lnT", [128, 1024], BF16),
            ("trash", [128, 512], F32),
            ("sbst", [128, (2 * TT) if h_rms else TT], F32),
        ]
        if h_rms:
            specs.append(("hsq", [128, 2 * TT], BF16))
        if cfg["ln_kind"] in ("l1ffn", "l0attn"):
            specs.append(("lnTf", [128, 1024], F32))
        if is_scratch:
            specs.append(("dtile", [128, 2 * XW], F32))
        ar = self.arena.layout(specs)
        ar.setdefault("lnTf", None)
        ar.setdefault("hsq", None)
        x2t, ht, hsq, urng, drng = ar["x2t"], ar["ht"], ar["hsq"], ar["urng"], ar["drng"]
        sbst = ar["sbst"]

        up_w, dn_w = cfg["up_w"], cfg["dn_w"]
        bank_free = {i: (p.mm, 0) for i in range(6)}
        bank_free["bb0"] = bank_free["bb1"] = (p.mm, 0)

        def bwait(eng, key):
            s, v = bank_free[key]
            p.wait(eng, s, v)

        uslot_mm = [0, 0]
        dslot_mm = [0, 0, 0]
        ucnt = [0]
        dcnt = [0]

        def load_up(fpair):
            i = ucnt[0]
            slot = i % 2
            if i == 0 and cfg.get("up_tok"):
                self.dqs.wait("sync", cfg["up_tok"])
            if i >= 2:
                p.wait("sync", p.mm, uslot_mm[slot])
            tk = self.dqs.dma(lambda e, fpair=fpair, slot=slot: e.dma_start(
                out=urng[:, slot * 2048:(slot + 1) * 2048]
                .rearrange("q (u w) -> q u w", u=2),
                in_=up_w[2 * fpair:2 * fpair + 2, :, :].rearrange("u q w -> q u w")))
            ucnt[0] += 1
            return slot, tk

        def load_dn(d, quarter):
            i = dcnt[0]
            slot = i % 3
            if i == 0 and cfg.get("dn_tok"):
                self.dqg.wait("gpsimd", cfg["dn_tok"])
            if i >= 3:
                p.wait("gpsimd", p.mm, dslot_mm[slot])
            tk = self.dqg.dma(lambda e, d=d, quarter=quarter, slot=slot: e.dma_start(
                out=drng[:, slot * 1024:(slot + 1) * 1024],
                in_=dn_w[d, :, quarter * 1024:(quarter + 1) * 1024]))
            dcnt[0] += 1
            return slot, tk

        dt_read_done = [0] * NT
        x2_read_done = [0] * NT
        dtoks = [None] * NT
        X2R = [0] * NT

        def emit_dt_load(t):
            half = t % 2
            dt = ar["dtile"]
            if t >= 2:
                assert dt_read_done[t - 2] > 0
                p.wait("sync", p.evV, dt_read_done[t - 2])
            toks = []
            for c in range(DC):
                toks.append(self.dqs.dma(lambda e, c=c, t=t, half=half: e.dma_start(
                    out=dt[:, half * XW + c * TT: half * XW + (c + 1) * TT],
                    in_=cfg["src_ap"][c, :, t * TT:(t + 1) * TT])))
            return toks

        ST = self.ST

        def emit_ln(t):
            col0 = t * TT
            half = t % 2
            if is_scratch:
                dt = ar["dtile"]
                srcf = (lambda c, j, half=half:
                        dt[:, half * XW + c * TT + j * 128: half * XW + c * TT + (j + 1) * 128])
            else:
                srcf = (lambda c, j, col0=col0:
                        xres[:, c * T + col0 + j * 128: c * T + col0 + (j + 1) * 128])
            lncfg = {
                "kind": cfg["ln_kind"],
                "src": srcf, "nj": NJ,
                "trash": ar["trash"], "lnT": ar["lnT"], "lnTf": ar.get("lnTf"),
                "bank_free": bank_free,
                "x2_out": (lambda c, j, half=half:
                           x2t[:, half * XW + c * TT + j * 128: half * XW + c * TT + (j + 1) * 128]),
                "x2_out_blk": (lambda j, half=half: x2t[:, half * XW: (half + 1) * XW]
                               .rearrange("q (c w) -> q c w", c=DC)[:, :, j * 128:(j + 1) * 128]),
                "xslice": (lambda c, j, col0=col0:
                           xres[:, c * T + col0 + j * 128: c * T + col0 + (j + 1) * 128]),
            }
            if t >= 2:
                lncfg["x2_write_wait"] = [(p.mm, x2_read_done[t - 2])]
            if is_scratch:
                lncfg["fwd_extra_wait"] = dtoks[t]
            if cfg["ln_kind"] == "lngb":
                lncfg["gam"], lncfg["bet"] = cfg["gam"], cfg["bet"]
            if cfg["ln_kind"] == "l1ffn":
                lncfg["gam"], lncfg["bet"] = self.V_LNG, self.V_LNB
            return self.emit_ln_tile(lncfg)

        def pump(g, n=1):
            if g is None:
                return
            for _ in range(n):
                try:
                    next(g)
                except StopIteration:
                    return

        if is_scratch:
            p.wait("sync", p.evV)
            for tok in cfg.get("src_ready", []):
                p.wait("sync", tok[0], tok[1])
            dtoks[0] = emit_dt_load(0)
            if NT > 1:
                dtoks[1] = emit_dt_load(1)
        pump(emit_ln(0), 100)
        X2R[0] = p.evV.n

        up_bias = cfg.get("up_bias")
        for t in range(NT):
            col0 = t * TT
            half = t % 2
            if cfg.get("pump_side"):
                self.pump_side(4)
            sq_evA = []
            ones_mm_vals = []

            def emit_ones_mm(c):
                if c == 0:
                    bwait("tensor", 4)
                p.wait("tensor", p.evA, sq_evA[c])
                p.ins("tensor", lambda e, c=c: e.matmul(
                    pb[4][0:1, 0:TT], self.onesb[:, 0:1],
                    hsq[:, (c % 2) * TT:(c % 2 + 1) * TT],
                    start=(c == 0), stop=(c == FC - 1)),
                    inc=(p.mm, 1))
                ones_mm_vals.append(p.mm.n)

            for f in range(FC):
                if f % 2 == 0:
                    slot, wl = load_up(f // 2)
                bank = 2 + (f % 2)
                p.wait("tensor", wl[0], wl[1])
                if f == 0:
                    p.wait("tensor", p.evV, X2R[t])
                bwait("tensor", bank)
                for k in range(DC):
                    p.ins("tensor", lambda e, slot=slot, k=k, f=f, bank=bank, half=half:
                          e.matmul(
                              pb[bank][:, 0:TT],
                              urng[:, slot * 2048 + (f % 2) * D + k * 128:
                                   slot * 2048 + (f % 2) * D + (k + 1) * 128],
                              x2t[:, half * XW + k * TT: half * XW + (k + 1) * TT],
                              start=(k == 0), stop=(k == DC - 1)),
                          inc=(p.mm, 1) if k == DC - 1 else None)
                uslot_mm[slot] = p.mm.n
                p.wait("scalar", p.mm)
                if up_bias is not None:
                    p.ins("scalar", lambda e, f=f, bank=bank: e.activation(
                        ht[:, f * TT:(f + 1) * TT], pb[bank][:, 0:TT], cfg["act"],
                        bias=vecs[:, up_bias + f:up_bias + f + 1]))
                else:
                    p.ins("scalar", lambda e, f=f, bank=bank: e.activation(
                        ht[:, f * TT:(f + 1) * TT], pb[bank][:, 0:TT], cfg["act"]))
                bank_free[bank] = (p.evA, p.evA.n)
                if h_rms:
                    if f >= 2:
                        p.wait("scalar", p.mm, ones_mm_vals[f - 2])
                    p.ins("scalar", lambda e, f=f: e.activation(
                        hsq[:, (f % 2) * TT:(f % 2 + 1) * TT],
                        ht[:, f * TT:(f + 1) * TT], ACTF.Square))
                    sq_evA.append(p.evA.n)
                    if f >= 1:
                        emit_ones_mm(f - 1)
            HT_READY = bank_free[2 + ((FC - 1) % 2)]
            x2_read_done[t] = p.mm.n

            SB_READY = None
            if h_rms:
                emit_ones_mm(FC - 1)
                ones_mm = p.mm.n
                p.wait("vector", p.mm, ones_mm)
                p.ins("vector", lambda e: e.tensor_scalar(
                    rows[64:65, 0:TT], pb[4][0:1, 0:TT], 1.0 / FF, None, ALU.mult))
                bank_free[4] = (p.evV, p.evV.n)
                p.wait("scalar", p.evV)
                p.ins("scalar", lambda e: e.activation(
                    rows[64:65, 0:TT], rows[64:65, 0:TT], ACTF.Sqrt,
                    bias=self.cpool[64:65, self.C_ZERO:self.C_ZERO + 1]))
                p.wait("vector", p.evA)
                p.ins("vector", lambda e: e.tensor_scalar(
                    rows[64:65, 0:TT], rows[64:65, 0:TT], 1e-8, None, ALU.add))
                p.ins("vector", lambda e: e.reciprocal(rows[64:65, TT:2 * TT],
                                                       rows[64:65, 0:TT]))
                p.ins("vector", lambda e: e.tensor_scalar(
                    rows[64:65, TT:2 * TT], rows[64:65, TT:2 * TT], 0.5, None, ALU.mult))
                p.wait("tensor", p.evV)
                bwait("tensor", 5)
                p.ins("tensor", lambda e: e.matmul(
                    pb[5][:, 0:TT], self.onesf[64:65, :], rows[64:65, TT:2 * TT],
                    start=True, stop=True), inc=(p.mm, 1))
                p.wait("vector", p.mm)
                p.ins("vector", lambda e: e.tensor_copy(sbst[:, TT:2 * TT], pb[5][:, 0:TT]))
                bank_free[5] = (p.evV, p.evV.n)
                SB_READY = p.evV.n

            gen = emit_ln(t + 1) if t + 1 < NT else None
            pump(gen, 2)

            om = cfg["out_mode"]
            for d in range(DC):
                bank = 2 + (d % 2)
                for dq in range(4):
                    slot, wl = load_dn(d, dq)
                    p.wait("tensor", wl[0], wl[1])
                    if d == 0 and dq == 0:
                        p.wait("tensor", HT_READY[0], HT_READY[1])
                    if dq == 0:
                        bwait("tensor", bank)
                    for f in range(8):
                        ff_ = dq * 8 + f
                        p.ins("tensor", lambda e, slot=slot, f=f, ff_=ff_, bank=bank: e.matmul(
                            pb[bank][:, 0:TT],
                            drng[:, slot * 1024 + f * 128: slot * 1024 + (f + 1) * 128],
                            ht[:, ff_ * TT:(ff_ + 1) * TT],
                            start=(ff_ == 0), stop=(ff_ == FC - 1)),
                            inc=(p.mm, 1) if f == 7 else None)
                    dslot_mm[slot] = p.mm.n
                if om == "half":
                    p.wait("vector", p.mm)
                    p.wait("vector", p.evV, SB_READY)
                    p.ins("vector", lambda e, bank=bank: e.tensor_mul(
                        sbst[:, 0:TT], pb[bank][:, 0:TT], sbst[:, TT:2 * TT]))
                    p.ins("vector", lambda e, d=d, col0=col0: e.tensor_add(
                        xres[:, d * T + col0: d * T + col0 + TT],
                        xres[:, d * T + col0: d * T + col0 + TT], sbst[:, 0:TT]))
                elif om == "quarter_bias":
                    p.wait("vector", p.mm)
                    p.ins("vector", lambda e, bank=bank, d=d: e.tensor_scalar(
                        sbst[:, 0:TT], pb[bank][:, 0:TT], 0.25,
                        vecs[:, self.V_QFC2B + d:self.V_QFC2B + d + 1],
                        ALU.mult, ALU.add))
                    p.ins("vector", lambda e, d=d, col0=col0: e.tensor_add(
                        xres[:, d * T + col0: d * T + col0 + TT],
                        xres[:, d * T + col0: d * T + col0 + TT], sbst[:, 0:TT]))
                elif om == "plain_bias_plus_src":
                    p.wait("vector", p.mm)
                    p.ins("vector", lambda e, bank=bank, d=d: e.tensor_scalar(
                        sbst[:, 0:TT], pb[bank][:, 0:TT], 1.0,
                        vecs[:, self.V_B2 + d:self.V_B2 + d + 1],
                        ALU.mult, ALU.add))
                    p.ins("vector", lambda e, d=d, col0=col0: e.tensor_add(
                        xres[:, d * T + col0: d * T + col0 + TT],
                        xres[:, d * T + col0: d * T + col0 + TT], sbst[:, 0:TT]))
                    p.ins("vector", lambda e, d=d, col0=col0, half=half: e.tensor_add(
                        xres[:, d * T + col0: d * T + col0 + TT],
                        xres[:, d * T + col0: d * T + col0 + TT],
                        ar["dtile"][:, half * XW + d * TT: half * XW + (d + 1) * TT]))
                bank_free[bank] = (p.evV, p.evV.n)
                pump(gen, 2)
            pump(gen, 100)
            if t + 1 < NT:
                X2R[t + 1] = p.evV.n
            dt_read_done[t] = p.evV.n
            if is_scratch and t + 2 < NT:
                dtoks[t + 2] = emit_dt_load(t + 2)
        p.fence()


    def emit_l0ffn(self):
        self.emit_ffn({
            "ln_kind": "l0ffn", "src": "xres",
            "up_w": self.ap["s_wup"], "dn_w": self.ap["s_wdn"],
            "act": ACTF.Gelu, "up_bias": None, "h_rms": True, "out_mode": "half",
        })

    def emit_l1ffn(self):
        self.emit_ffn({
            "ln_kind": "l1ffn", "src": "xres",
            "up_w": self.ap["s_fc1"], "dn_w": self.ap["s_fc2"],
            "act": ACTF.Silu, "up_bias": self.V_FC1B, "h_rms": False,
            "out_mode": "quarter_bias",
        })

    def emit_alpha(self, kind, gam=None, bet=None, want_s=False):
        p, ap = self.p, self.ap
        W = DC * 512
        ar = self.arena.layout([
            ("x2s", [128, 2 * W], BF16),
            ("ss", [128, 2 * W], F32),
            ("lnT", [128, 2 * 1024], BF16),
            ("lnTf", [128, 2 * 1024], F32),
            ("trash", [128, 512], F32),
        ])
        x2s, ss = ar["x2s"], ar["ss"]
        bank_free = {i: (self.p.mm, 0) for i in range(6)}
        bank_free["bb0"] = bank_free["bb1"] = (self.p.mm, 0)
        bbctr = [0]
        toks = []
        out_toks = {}

        def make_gen(t):
            col0 = t * 512
            pr = t % 2
            lncfg = {
                "kind": kind, "nj": 4,
                "bank_free": bank_free,
                "fb_pair": (0, 1) if pr == 0 else (2, 3),
                "stat_base": 0 if pr == 0 else 16,
                "bbctr": bbctr,
                "src": (lambda c, j, col0=col0:
                        self.xres[:, c * T + col0 + j * 128: c * T + col0 + (j + 1) * 128]),
                "trash": ar["trash"],
                "lnT": ar["lnT"][:, pr * 1024:(pr + 1) * 1024],
                "lnTf": ar["lnTf"][:, pr * 1024:(pr + 1) * 1024],
                "x2_out": (lambda c, j, pr=pr:
                           x2s[:, pr * W + c * 512 + j * 128: pr * W + c * 512 + (j + 1) * 128]),
                "x2_out_blk": (lambda j, pr=pr: x2s[:, pr * W:(pr + 1) * W]
                               .rearrange("q (c w) -> q c w", c=DC)[:, :, j * 128:(j + 1) * 128]),
                "s_out": (lambda c, j, pr=pr:
                          ss[:, pr * W + c * 512 + j * 128: pr * W + c * 512 + (j + 1) * 128]),
            }
            if t >= 2 and (t - 2) in out_toks:
                lncfg["x2_write_wait"] = out_toks[t - 2]
            if gam is not None:
                lncfg["gam"], lncfg["bet"] = gam, bet
            return self.emit_ln_tile(lncfg)

        def pump1(g):
            try:
                next(g)
                return True
            except StopIteration:
                return False

        def emit_outs(t):
            col0 = t * 512
            pr = t % 2
            p.wait("sync", p.evV)
            tk1 = self.dqs.dma(lambda e, col0=col0, pr=pr: e.dma_start(
                out=ap["s_x2"][:, :, col0:col0 + 512].rearrange("c q w -> q c w"),
                in_=x2s[:, pr * W:(pr + 1) * W].rearrange("q (c w) -> q c w", c=DC)))
            toks.append(tk1)
            if want_s:
                tk2 = self.dqs.dma(lambda e, col0=col0, pr=pr: e.dma_start(
                    out=ap["s_s"][:, :, col0:col0 + 512].rearrange("c q w -> q c w"),
                    in_=ss[:, pr * W:(pr + 1) * W].rearrange("q (c w) -> q c w", c=DC)))
                toks.append(tk2)
                out_toks[t] = [tk1, tk2]
            else:
                out_toks[t] = [tk1]

        qd = []
        for t in range(T // 512):
            qd.append((t, make_gen(t)))
            if len(qd) == 2:
                (t0, g0), (t1, g1) = qd
                alive = True
                while alive:
                    alive = pump1(g0)
                    pump1(g1)
                emit_outs(t0)
                qd.pop(0)
        tl, gl = qd[0]
        while pump1(gl):
            pass
        emit_outs(tl)
        return toks

    def emit_l0attn(self):
        p, ap, pb = self.p, self.ap, self.pb
        vecs, onesf, onesb = self.vecs, self.onesf, self.onesb
        if 4 in self.stages or 6 in self.stages:
            HICQ = 128
            dq_ar = self.arena.reserve_tail([
                ("qi", [128, 3 * HICQ], I32),
                ("qf", [128, 2 * HICQ], F32),
                ("qs", [128, 2 * HICQ], F32),
                ("qo", [128, 3 * HICQ], BF16),
            ])
            self.side_res = dq_ar["_need"]
            self.side_work.append(self._deq(("fc1_p", "fc1s_p", "s_fc1", FC, D, HICQ), dq_ar))
            self.side_work.append(self._deq(("fc2_p", "fc2s_p", "s_fc2", DC, FF, HICQ), dq_ar))
        a_toks = self.emit_alpha("l0attn", self.V_G1, self.V_BE1, want_s=True)
        p.fence()

        dmsg_toks = []
        for s in range(SEQ_PER_CORE):
            soff = s * S
            for hg in range(2):
                ar = self.arena.layout([
                    ("kvacc", [128, 4 * 129], F32),
                    ("kvbf", [128, 4 * 129], BF16),
                    ("kbuf", [128, 16 * 512], BF16),
                    ("vbuf", [128, 16 * 512], BF16),
                    ("x2t", [128, DC * 512], BF16),
                    ("wkb", [128, DC * 512], BF16),
                    ("wvb", [128, DC * 512], BF16),
                    ("est", [128, 4 * 512], F32),
                ])
                kbuf, vbuf, x2t = ar["kbuf"], ar["vbuf"], ar["x2t"]
                wkb, wvb, est = ar["wkb"], ar["wvb"], ar["est"]
                kvacc, kvbf = ar["kvacc"], ar["kvbf"]
                p.wait("sync", p.mm)
                p.wait("sync", p.evV)
                p.wait("sync", p.evA)
                self.dqs.drain("sync")
                if "s_wk" in self.cast_tok:
                    self.dqs.wait("sync", self.cast_tok["s_wk"])
                    self.dqs.wait("sync", self.cast_tok["s_wv"])
                tkk = self.dqs.dma(lambda e, hg=hg: e.dma_start(
                    out=wkb[:, :].rearrange("q (c w) -> q c w", c=DC),
                    in_=ap["s_wk"][:, :, hg * 512:(hg + 1) * 512].rearrange("c q w -> q c w")))
                tkv = self.dqs.dma(lambda e, hg=hg: e.dma_start(
                    out=wvb[:, :].rearrange("q (c w) -> q c w", c=DC),
                    in_=ap["s_wv"][:, :, hg * 512:(hg + 1) * 512].rearrange("c q w -> q c w")))
                self.dqs.wait("tensor", tkk)
                self.dqs.wait("tensor", tkv)
                flushed = 0
                kvfree = {0: (p.mm, 0), 1: (p.mm, 0), 2: (p.mm, 0), 3: (p.mm, 0),
                          4: (p.mm, 0), 5: (p.mm, 0)}
                for tt in range(4):
                    col0 = soff + tt * 512
                    self.pump_side(12)
                    p.wait("sync", p.mm)
                    tkx = self.dqs.dma(lambda e, col0=col0: e.dma_start(
                        out=x2t[:, :].rearrange("q (c w) -> q c w", c=DC),
                        in_=ap["s_x2"][:, :, col0:col0 + 512].rearrange("c q w -> q c w")))
                    self.dqs.wait("tensor", tkx)
                    for i in range(4):
                        tc = tt * 4 + i
                        kb_bank = (2 * tc) % 4
                        vb_bank = (2 * tc + 1) % 4
                        pr = tc % 2
                        p.wait("tensor", kvfree[kb_bank][0], kvfree[kb_bank][1])
                        for c in range(DC):
                            p.ins("tensor", lambda e, c=c, i=i, kb_bank=kb_bank: e.matmul(
                                pb[kb_bank][:, :],
                                x2t[:, c * 512 + i * 128: c * 512 + (i + 1) * 128],
                                wkb[:, c * 512:(c + 1) * 512],
                                start=(c == 0), stop=(c == DC - 1)),
                                inc=(p.mm, 1) if c == DC - 1 else None)
                        kmm = p.mm.n
                        p.wait("tensor", kvfree[vb_bank][0], kvfree[vb_bank][1])
                        for c in range(DC):
                            p.ins("tensor", lambda e, c=c, i=i, vb_bank=vb_bank: e.matmul(
                                pb[vb_bank][:, :],
                                x2t[:, c * 512 + i * 128: c * 512 + (i + 1) * 128],
                                wvb[:, c * 512:(c + 1) * 512],
                                start=(c == 0), stop=(c == DC - 1)),
                                inc=(p.mm, 1) if c == DC - 1 else None)
                        vmm = p.mm.n
                        p.wait("vector", p.mm, kmm)
                        p.ins("vector", lambda e, kb_bank=kb_bank, pr=pr: e.tensor_scalar(
                            est[:, pr * 1024: pr * 1024 + 512], pb[kb_bank][:, :],
                            0.0, None, ALU.min))
                        p.wait("scalar", p.evV, p.evV.n)
                        p.ins("scalar", lambda e, pr=pr: e.activation(
                            est[:, pr * 1024 + 512: pr * 1024 + 1024],
                            est[:, pr * 1024: pr * 1024 + 512], ACTF.Exp))
                        p.wait("vector", p.evA, p.evA.n)
                        p.ins("vector", lambda e, tc=tc, kb_bank=kb_bank, pr=pr:
                              e.scalar_tensor_tensor(
                                  kbuf[:, tc * 512:(tc + 1) * 512], pb[kb_bank][:, :], 0.0,
                                  est[:, pr * 1024 + 512: pr * 1024 + 1024],
                                  ALU.max, ALU.add))
                        kvfree[kb_bank] = (p.evV, p.evV.n)
                        p.wait("vector", p.mm, vmm)
                        p.ins("vector", lambda e, tc=tc, vb_bank=vb_bank: e.tensor_copy(
                            vbuf[:, tc * 512:(tc + 1) * 512], pb[vb_bank][:, :]))
                        kvfree[vb_bank] = (p.evV, p.evV.n)
                    p.wait("tensor", p.evV)
                    for h in range(4):
                        kv_bank = 4 + (h % 2)
                        p.wait("tensor", kvfree[kv_bank][0], kvfree[kv_bank][1])
                        nmm = 0
                        for i in range(4):
                            tc = tt * 4 + i
                            for (colr, rhs) in ((slice(0, 128),
                                                 lambda e, tc=tc, h=h: vbuf[:, tc * 512 + h * 128: tc * 512 + (h + 1) * 128]),
                                                (slice(128, 129),
                                                 lambda e: onesb[:, 0:1])):
                                nmm += 1
                                p.ins("tensor", lambda e, tc=tc, h=h, colr=colr, rhs=rhs,
                                      kv_bank=kv_bank, nmm=nmm: e.matmul(
                                          pb[kv_bank][:, colr],
                                          kbuf[:, tc * 512 + h * 128: tc * 512 + (h + 1) * 128],
                                          rhs(e),
                                          start=(nmm == 1), stop=(nmm == 8)),
                                      inc=(p.mm, 1) if nmm == 8 else None)
                        p.wait("vector", p.mm)
                        if tt == 0:
                            p.ins("vector", lambda e, h=h, kv_bank=kv_bank: e.tensor_copy(
                                kvacc[:, h * 129:(h + 1) * 129], pb[kv_bank][:, 0:129]))
                        else:
                            p.ins("vector", lambda e, h=h, kv_bank=kv_bank: e.tensor_add(
                                kvacc[:, h * 129:(h + 1) * 129],
                                kvacc[:, h * 129:(h + 1) * 129], pb[kv_bank][:, 0:129]))
                        kvfree[kv_bank] = (p.evV, p.evV.n)
                p.ins("vector", lambda e: e.tensor_copy(kvbf[:, :], kvacc[:, :]))
                KV_READY = p.evV.n

                ar2 = self.arena.layout([
                    ("kvacc2", [128, 4 * 129], F32),
                    ("kvbf2", [128, 4 * 129], BF16),
                    ("qt", [128, 4 * 512], BF16),
                    ("x2q", [128, DC * 512], BF16),
                    ("wqb", [128, DC * 512], BF16),
                    ("stile", [128, 4 * 512], F32),
                    ("zq", [128, 2 * 512], F32),
                    ("zbs", [128, 2 * 512], F32),
                    ("dst", [128, 2 * 512], F32),
                    ("eq", [128, 4 * 512], F32),
                ])
                qt, x2q, wqb, stile = ar2["qt"], ar2["x2q"], ar2["wqb"], ar2["stile"]
                zq, zbs, dst, eq = ar2["zq"], ar2["zbs"], ar2["dst"], ar2["eq"]
                p.wait("sync", p.mm)
                p.wait("sync", p.evV)
                self.dqs.drain("sync")
                if "s_wq" in self.cast_tok:
                    self.dqs.wait("sync", self.cast_tok["s_wq"])
                tkw = self.dqs.dma(lambda e, hg=hg: e.dma_start(
                    out=wqb[:, :].rearrange("q (c w) -> q c w", c=DC),
                    in_=ap["s_wq"][:, :, hg * 512:(hg + 1) * 512].rearrange("c q w -> q c w")))
                self.dqs.wait("tensor", tkw)
                for lt in range(4):
                    col0 = soff + lt * 512
                    p.wait("sync", p.mm)
                    p.wait("sync", p.evV)
                    tkx = self.dqs.dma(lambda e, col0=col0: e.dma_start(
                        out=x2q[:, :].rearrange("q (c w) -> q c w", c=DC),
                        in_=ap["s_x2"][:, :, col0:col0 + 512].rearrange("c q w -> q c w")))
                    tks = self.dqs.dma(lambda e, col0=col0, hg=hg: e.dma_start(
                        out=stile[:, :].rearrange("q (c w) -> q c w", c=4),
                        in_=ap["s_s"][hg * 4:(hg + 1) * 4, :, col0:col0 + 512]
                        .rearrange("c q w -> q c w")))
                    self.dqs.wait("tensor", tkx)
                    qfree = {0: (p.mm, 0), 1: (p.mm, 0)}
                    for r in range(4):
                        bank = 0 + (r % 2)
                        p.wait("tensor", qfree[bank][0], qfree[bank][1])
                        for c in range(DC):
                            p.ins("tensor", lambda e, c=c, r=r, hg=hg, bank=bank: e.matmul(
                                pb[bank][:, :],
                                wqb[:, c * 512 + r * 128: c * 512 + (r + 1) * 128],
                                x2q[:, c * 512:(c + 1) * 512],
                                start=(c == 0), stop=(c == DC - 1)),
                                inc=(p.mm, 1) if c == DC - 1 else None)
                        qmm = p.mm.n
                        p.wait("vector", p.mm, qmm)
                        pr = r % 2
                        p.ins("vector", lambda e, bank=bank, pr=pr: e.tensor_scalar(
                            eq[:, pr * 1024: pr * 1024 + 512], pb[bank][:, :],
                            0.0, None, ALU.min))
                        p.wait("scalar", p.evV, p.evV.n)
                        p.ins("scalar", lambda e, pr=pr: e.activation(
                            eq[:, pr * 1024 + 512: pr * 1024 + 1024],
                            eq[:, pr * 1024: pr * 1024 + 512], ACTF.Exp))
                        p.wait("vector", p.evA, p.evA.n)
                        p.ins("vector", lambda e, r=r, bank=bank, pr=pr:
                              e.scalar_tensor_tensor(
                                  qt[:, r * 512:(r + 1) * 512], pb[bank][:, :], 0.0,
                                  eq[:, pr * 1024 + 512: pr * 1024 + 1024],
                                  ALU.max, ALU.add))
                        qfree[bank] = (p.evV, p.evV.n)
                    Q_READY = p.evV.n
                    self.dqs.wait("vector", tks)
                    mfree = {0: (p.evV, qfree[0][1]), 1: (p.evV, qfree[1][1]),
                             2: (p.mm, 0), 3: (p.mm, 0), 4: (p.mm, 0), 5: (p.mm, 0)}
                    pend = [None]

                    def l0_tail():
                        if pend[0] is None:
                            return
                        hh, mb, db, zb_bank, recV = pend[0]
                        pend[0] = None
                        pr = hh % 2
                        p.wait("tensor", p.evV, recV)
                        p.wait("tensor", mfree[zb_bank][0], mfree[zb_bank][1])
                        p.ins("tensor", lambda e, zb_bank=zb_bank, pr=pr: e.matmul(
                            pb[zb_bank][:, :], onesf[0:1, :], zq[0:1, pr * 512:(pr + 1) * 512],
                            start=True, stop=True), inc=(p.mm, 1))
                        p.wait("vector", p.mm, p.mm.n)
                        p.ins("vector", lambda e, zb_bank=zb_bank, pr=pr: e.tensor_copy(
                            zbs[:, pr * 512:(pr + 1) * 512], pb[zb_bank][:, :]))
                        mfree[zb_bank] = (p.evV, p.evV.n)
                        if len(dmsg_toks) >= 2:
                            self.dqs.wait("vector", dmsg_toks[-2])
                        p.ins("vector", lambda e, mb=mb, pr=pr: e.tensor_mul(
                            dst[:, pr * 512:(pr + 1) * 512], pb[mb][:, :],
                            zbs[:, pr * 512:(pr + 1) * 512]))
                        mfree[mb] = (p.evV, p.evV.n)
                        p.ins("vector", lambda e, hh=hh, pr=pr: e.tensor_add(
                            dst[:, pr * 512:(pr + 1) * 512],
                            dst[:, pr * 512:(pr + 1) * 512],
                            stile[:, hh * 512:(hh + 1) * 512]))
                        p.wait("sync", p.evV)
                        dmsg_toks.append(self.dqs.dma(
                            lambda e, hh=hh, hg=hg, col0=col0, pr=pr: e.dma_start(
                                out=ap["s_dmsg"][hg * 4 + hh, :, col0:col0 + 512],
                                in_=dst[:, pr * 512:(pr + 1) * 512])))

                    for h in range(4):
                        mb = 2 + (h % 2)
                        db = 0 + (h % 2)
                        zb_bank = 4 + (h % 2)
                        pr = h % 2
                        if h == 0:
                            p.wait("tensor", p.evV, max(KV_READY, Q_READY))
                        p.wait("tensor", mfree[mb][0], mfree[mb][1])
                        p.ins("tensor", lambda e, h=h, mb=mb: e.matmul(
                            pb[mb][:, :], kvbf[:, h * 129: h * 129 + 128],
                            qt[:, h * 512:(h + 1) * 512], start=True, stop=True),
                            inc=(p.mm, 1))
                        p.wait("tensor", mfree[db][0], mfree[db][1])
                        p.ins("tensor", lambda e, h=h, db=db: e.matmul(
                            pb[db][0:1, :], kvbf[:, h * 129 + 128: h * 129 + 129],
                            qt[:, h * 512:(h + 1) * 512], start=True, stop=True),
                            inc=(p.mm, 1))
                        mmv = p.mm.n
                        l0_tail()
                        p.wait("vector", p.mm, mmv)
                        p.ins("vector", lambda e, db=db, pr=pr: e.tensor_scalar(
                            zq[0:1, pr * 512:(pr + 1) * 512], pb[db][0:1, :],
                            1e-6, None, ALU.add))
                        mfree[db] = (p.evV, p.evV.n)
                        p.ins("vector", lambda e, pr=pr: e.reciprocal(
                            zq[0:1, pr * 512:(pr + 1) * 512],
                            zq[0:1, pr * 512:(pr + 1) * 512]))
                        pend[0] = (h, mb, db, zb_bank, p.evV.n)
                    l0_tail()
        p.fence()


        self.emit_l0attn_inner()

    def emit_l0attn_inner(self):
        self.emit_ffn({
            "ln_kind": "lngb", "gam": self.V_G2, "bet": self.V_BE2,
            "src": "scratch", "src_ap": self.ap["s_dmsg"], "src_ready": [],
            "up_tok": self.cast_tok.get("s_w1"), "dn_tok": self.cast_tok.get("s_w2"),
            "up_w": self.ap["s_w1"], "dn_w": self.ap["s_w2"],
            "act": ACTF.Gelu, "up_bias": self.V_B1, "h_rms": False,
            "pump_side": True,
            "out_mode": DBG_OM[0] if DBG_OM else "plain_bias_plus_src", "tt": 256,
            **({"nt_limit": DBG_NT_LIMIT[0]} if DBG_NT_LIMIT else {}),
        })
        self.drain_side()
        self.p.fence()

    def emit_l1attn(self):
        p, ap, pb = self.p, self.ap, self.pb
        vecs, onesf, onesb, maskb, rows = self.vecs, self.onesf, self.onesb, self.maskb, self.rows
        self.emit_alpha("plain", want_s=False)
        p.fence()

        NH = 8
        for s in range(SEQ_PER_CORE):
            soff = s * S
            for H in range(2):
                ar = self.arena.layout([
                    ("ktb", [128, 4 * 2048], BF16),
                    ("vb", [128, 16 * 520], BF16),
                    ("x2t", [128, DC * 512], BF16),
                    ("wb", [128, DC * 512], BF16),
                    ("qb", [128, 4 * 512], BF16),
                    ("msgb", [128, 8 * 512], BF16),
                    ("pt", [128, 6 * 512], BF16),
                    ("wob", [128, 2 * 1024], BF16),
                    ("zbs", [128, 512], F32),
                ])
                ktb, vb, x2t, wb, qb = ar["ktb"], ar["vb"], ar["x2t"], ar["wb"], ar["qb"]
                msgb, ptb, wob = ar["msgb"], ar["pt"], ar["wob"]
                zbs = ar["zbs"]
                zst = zbs
                p.wait("sync", p.mm)
                p.wait("sync", p.evV)
                p.wait("sync", p.evA)
                self.dqs.drain("sync")
                p.ins("vector", lambda e: e.memset(msgb[64:128, :], 0.0))

                for nm in ("s_Wk", "s_Wv", "s_Wq", "s_Wo"):
                    if nm in self.cast_tok:
                        self.dqs.wait("sync", self.cast_tok[nm])
                tkw = self.dqs.dma(lambda e, H=H: e.dma_start(
                    out=wb[:, :].rearrange("q (c w) -> q c w", c=DC),
                    in_=ap["s_Wk"][:, :, H * 512:(H + 1) * 512].rearrange("c q w -> q c w")))
                self.dqs.wait("tensor", tkw)
                kfree = {i: (p.mm, 0) for i in range(6)}
                for tt in range(4):
                    col0 = soff + tt * 512
                    p.wait("sync", p.mm)
                    tkx = self.dqs.dma(lambda e, col0=col0: e.dma_start(
                        out=x2t[:, :].rearrange("q (c w) -> q c w", c=DC),
                        in_=ap["s_x2"][:, :, col0:col0 + 512].rearrange("c q w -> q c w")))
                    self.dqs.wait("tensor", tkx)
                    for r in range(4):
                        bank = (tt * 4 + r) % 4
                        p.wait("tensor", kfree[bank][0], kfree[bank][1])
                        for c in range(DC):
                            p.ins("tensor", lambda e, c=c, r=r, bank=bank: e.matmul(
                                pb[bank][:, :],
                                wb[:, c * 512 + r * 128: c * 512 + (r + 1) * 128],
                                x2t[:, c * 512:(c + 1) * 512],
                                start=(c == 0), stop=(c == DC - 1)),
                                inc=(p.mm, 1) if c == DC - 1 else None)
                        p.wait("vector", p.mm, p.mm.n)
                        p.ins("vector", lambda e, r=r, tt=tt, H=H, bank=bank: e.tensor_scalar(
                            ktb[:, r * 2048 + tt * 512: r * 2048 + (tt + 1) * 512],
                            pb[bank][:, :], 1.0,
                            vecs[:, self.V_BK + H * 4 + r: self.V_BK + H * 4 + r + 1],
                            ALU.mult, ALU.add))
                        kfree[bank] = (p.evV, p.evV.n)
                p.wait("sync", p.mm)
                tkw = self.dqs.dma(lambda e, H=H: e.dma_start(
                    out=wb[:, :].rearrange("q (c w) -> q c w", c=DC),
                    in_=ap["s_Wv"][:, :, H * 512:(H + 1) * 512].rearrange("c q w -> q c w")))
                self.dqs.wait("tensor", tkw)
                for h in range(NH):
                    p.ins("vector", lambda e, h=h: e.memset(
                        vb[:, :].rearrange("q (tc w) -> q tc w", tc=16)[:, :, h * 65 + 64: h * 65 + 65],
                        1.0))
                for tt in range(4):
                    col0 = soff + tt * 512
                    p.wait("sync", p.mm)
                    tkx = self.dqs.dma(lambda e, col0=col0: e.dma_start(
                        out=x2t[:, :].rearrange("q (c w) -> q c w", c=DC),
                        in_=ap["s_x2"][:, :, col0:col0 + 512].rearrange("c q w -> q c w")))
                    self.dqs.wait("tensor", tkx)
                    for i in range(4):
                        tc = tt * 4 + i
                        bank = tc % 4
                        p.wait("tensor", kfree[bank][0], kfree[bank][1])
                        for c in range(DC):
                            p.ins("tensor", lambda e, c=c, i=i, bank=bank: e.matmul(
                                pb[bank][:, :],
                                x2t[:, c * 512 + i * 128: c * 512 + (i + 1) * 128],
                                wb[:, c * 512:(c + 1) * 512],
                                start=(c == 0), stop=False),
                                inc=None)
                        p.ins("tensor", lambda e, i=i, H=H, bank=bank: e.matmul(
                            pb[bank][:, :], onesf[0:1, 0:128],
                            rows[0:1, H * 512:(H + 1) * 512],
                            start=False, stop=True), inc=(p.mm, 1))
                        p.wait("vector", p.mm, p.mm.n)
                        p.ins("vector", lambda e, tc=tc, bank=bank: e.tensor_copy(
                            vb[:, tc * 520: tc * 520 + 520]
                            .rearrange("q (h w) -> q h w", h=NH)[:, :, 0:64],
                            pb[bank][:, :].rearrange("q (h w) -> q h w", h=NH)))
                        kfree[bank] = (p.evV, p.evV.n)
                p.wait("sync", p.mm)
                tkwq = self.dqs.dma(lambda e, H=H: e.dma_start(
                    out=wb[:, :].rearrange("q (c w) -> q c w", c=DC),
                    in_=ap["s_Wq"][:, :, H * 512:(H + 1) * 512].rearrange("c q w -> q c w")))
                self.dqs.wait("tensor", tkwq)
                for qt in range(4):
                    col0 = soff + qt * 512
                    p.wait("sync", p.mm)
                    tkx = self.dqs.dma(lambda e, col0=col0: e.dma_start(
                        out=x2t[:, :].rearrange("q (c w) -> q c w", c=DC),
                        in_=ap["s_x2"][:, :, col0:col0 + 512].rearrange("c q w -> q c w")))
                    self.dqs.wait("tensor", tkx)
                    for r in range(4):
                        bank = r % 4
                        p.wait("tensor", kfree[bank][0], kfree[bank][1])
                        for c in range(DC):
                            p.ins("tensor", lambda e, c=c, r=r, bank=bank: e.matmul(
                                pb[bank][:, :],
                                wb[:, c * 512 + r * 128: c * 512 + (r + 1) * 128],
                                x2t[:, c * 512:(c + 1) * 512],
                                start=(c == 0), stop=(c == DC - 1)),
                                inc=(p.mm, 1) if c == DC - 1 else None)
                        p.wait("vector", p.mm, p.mm.n)
                        p.ins("vector", lambda e, r=r, H=H, bank=bank: e.tensor_scalar(
                            qb[:, r * 512:(r + 1) * 512], pb[bank][:, :], 1.0,
                            vecs[:, self.V_BQ + H * 4 + r: self.V_BQ + H * 4 + r + 1],
                            ALU.mult, ALU.add))
                        kfree[bank] = (p.evV, p.evV.n)
                    QB_READY = p.evV.n
                    if qt <= 1:
                        kts = [(kt, 0, False) for kt in range(8)]
                    elif qt == 2:
                        kts = [(kt, 0, False) for kt in range(8)] + \
                              [(8 + i, i * 128, True) for i in range(4)]
                    else:
                        kts = [(kt, 0, False) for kt in range(12)] + \
                              [(12 + i, i * 128, True) for i in range(4)]
                    srot = []
                    avrot = {}
                    o_free = {4: (p.mm, 0), 5: (p.mm, 0)}
                    skt = [0]
                    pending = [None]

                    def emit_sbank_mm():
                        i = skt[0]
                        sb = i % 4
                        ps = i % 6
                        if i >= 4:
                            p.wait("tensor", srot[i - 4][0], srot[i - 4][1])
                        elif i == 0:
                            p.wait("tensor", p.evV, QB_READY)
                        skt[0] += 1
                        return sb, ps, i

                    def flush_tail():
                        if pending[0] is None:
                            return
                        hh, ob, rv = pending[0]
                        pending[0] = None
                        sb, _, _i = emit_sbank_mm()
                        avrot[_i] = 0
                        p.wait("tensor", p.evV, rv)
                        p.ins("tensor", lambda e, sb=sb: e.matmul(
                            pb[sb][:, :], onesf[64:65, :], zst[64:65, :],
                            start=True, stop=True), inc=(p.mm, 1))
                        p.wait("vector", p.mm, p.mm.n)
                        p.ins("vector", lambda e: e.tensor_copy(zbs[:, :], pb[sb][:, :]))
                        srot.append((p.evV, p.evV.n))
                        kfree[sb] = (p.evV, p.evV.n)
                        p.ins("vector", lambda e, hh=hh, ob=ob: e.tensor_mul(
                            msgb[0:64, hh * 512:(hh + 1) * 512],
                            pb[ob][0:64, :], zbs[0:64, :]))
                        o_free[ob] = (p.evV, p.evV.n)

                    av_q = [None]

                    def emit_av():
                        if av_q[0] is None:
                            return
                        fn = av_q[0]
                        av_q[0] = None
                        fn()

                    for h in range(NH):
                        r, bp = h // 2, (h % 2) * 64
                        ob = 4 + (h % 2)
                        av_last = [0]
                        for n_kt, (kt, live0, diag) in enumerate(kts):
                            sb, ps, ii = emit_sbank_mm()
                            p.ins("tensor", lambda e, kt=kt, live0=live0, r=r, bp=bp,
                                  sb=sb: e.matmul(
                                      pb[sb][:, live0:512],
                                      ktb[bp:bp + 64, r * 2048 + kt * 128: r * 2048 + (kt + 1) * 128],
                                      qb[bp:bp + 64, r * 512 + live0: r * 512 + 512],
                                      start=True, stop=True), inc=(p.mm, 1))
                            smm = p.mm.n
                            if n_kt == 0:
                                p.wait("tensor", o_free[ob][0], o_free[ob][1])
                            emit_av()
                            if n_kt == 2:
                                flush_tail()
                            p.wait("scalar", p.mm, smm)
                            if ii >= 6 and avrot.get(ii - 6):
                                p.wait("scalar", p.mm, avrot[ii - 6])
                            p.ins("scalar", lambda e, ps=ps, live0=live0, sb=sb:
                                  e.activation(
                                      ptb[:, ps * 512 + live0: (ps + 1) * 512],
                                      pb[sb][:, live0:512], ACTF.Exp,
                                      bias=self.cp(self.C_EXPB), scale=self.cp(self.C_QK)))
                            srot.append((p.evA, p.evA.n))
                            exp_evA = p.evA.n
                            mask_evV = None
                            if diag:
                                p.wait("vector", p.evA, exp_evA)
                                p.ins("vector", lambda e, ps=ps, live0=live0: e.tensor_mul(
                                    ptb[:, ps * 512 + live0: ps * 512 + live0 + 128],
                                    ptb[:, ps * 512 + live0: ps * 512 + live0 + 128],
                                    maskb[:, :]))
                                mask_evV = p.evV.n

                            def mk_av(kt=kt, h=h, ps=ps, live0=live0, n_kt=n_kt,
                                      nk=len(kts), ob=ob, exp_evA=exp_evA,
                                      mask_evV=mask_evV, ii=ii):
                                if mask_evV is not None:
                                    p.wait("tensor", p.evV, mask_evV)
                                else:
                                    p.wait("tensor", p.evA, exp_evA)
                                p.ins("tensor", lambda e: e.matmul(
                                    pb[ob][0:65, live0:512],
                                    vb[:, kt * 520 + h * 65: kt * 520 + (h + 1) * 65],
                                    ptb[:, ps * 512 + live0: (ps + 1) * 512],
                                    start=(n_kt == 0), stop=(n_kt == nk - 1)),
                                    inc=(p.mm, 1))
                                av_last[0] = p.mm.n
                                avrot[ii] = p.mm.n
                            av_q[0] = mk_av
                        emit_av()
                        flush_tail()
                        p.wait("vector", p.mm, av_last[0])
                        p.ins("vector", lambda e, ob=ob: e.reciprocal(
                            zst[64:65, :], pb[ob][64:65, :]))
                        pending[0] = (h, ob, p.evV.n)
                    flush_tail()
                    MSG_READY = p.evV.n
                    wo_mm = [0, 0]
                    for d in range(DC):
                        bank = 0 + (d % 2)
                        slot = d % 2
                        if d >= 2:
                            p.wait("sync", p.mm, wo_mm[slot])
                        tkwo = self.dqs.dma(lambda e, d=d, H=H, slot=slot: e.dma_start(
                            out=wob[:, slot * 1024:(slot + 1) * 1024],
                            in_=ap["s_Wo"][d, :, H * 1024:(H + 1) * 1024]))
                        p.wait("tensor", p.evV, MSG_READY)
                        self.dqs.wait("tensor", tkwo)
                        p.wait("tensor", kfree[bank][0], kfree[bank][1])
                        for ci in range(8):
                            p.ins("tensor", lambda e, ci=ci, d=d, bank=bank, slot=slot: e.matmul(
                                pb[bank][:, :],
                                wob[:, slot * 1024 + ci * 128: slot * 1024 + (ci + 1) * 128],
                                msgb[:, ci * 512:(ci + 1) * 512],
                                start=(ci == 0), stop=(ci == 7)),
                                inc=(p.mm, 1) if ci == 7 else None)
                        wo_mm[slot] = p.mm.n
                        p.wait("vector", p.mm)
                        p.ins("vector", lambda e, d=d, col0=col0, bank=bank: e.tensor_add(
                            self.xres[:, d * T + col0: d * T + col0 + 512],
                            self.xres[:, d * T + col0: d * T + col0 + 512],
                            pb[bank][:, :]))
                        kfree[bank] = (p.evV, p.evV.n)
        p.fence()

    def emit_store(self):
        p = self.p
        p.fence()
        for c in range(DC):
            self.dqs.dma(lambda e, c=c: e.dma_start(
                out=self.yT[c, :, :], in_=self.xres[:, c * T:(c + 1) * T]))
        self.dqs.drain("sync")
        self.dqg.drain("sync")

    def realize(self):
        p = self.p
        with self.nc.Block() as block:
            @block.tensor
            def _(e):
                for fn in p.streams["tensor"]:
                    fn(e)

            @block.vector
            def _(e):
                for fn in p.streams["vector"]:
                    fn(e)

            @block.scalar
            def _(e):
                for fn in p.streams["scalar"]:
                    fn(e)

            @block.gpsimd
            def _(e):
                for fn in p.streams["gpsimd"]:
                    fn(e)

            @block.sync
            def _(e):
                for fn in p.streams["sync"]:
                    fn(e)
        return self.nc


def build_nc(stages=STAGES):
    b = Build(stages)
    b.emit_init()
    b.emit_prep()
    for s in sorted(stages):
        if s in (1, 3):
            b.emit_l0ffn()
        elif s == 2:
            b.emit_l0attn()
        elif s in (4, 6):
            b.emit_l1ffn()
        elif s == 5:
            b.emit_l1attn()
        elif s == 7:
            b.emit_l0attn_inner()
    b.emit_store()
    return b.realize()



def kernel(**inputs):
    d = inputs
    f32 = np.float32

    shared = {}
    shared["wup_p"] = pack_w_outmajor(np.asarray(d["L0_ffn_w_up"], f32))
    shared["wdn_p"] = pack_w_outmajor(np.asarray(d["L0_ffn_w_down"], f32))
    shared["w1_p"] = pack_w_outmajor(np.asarray(d["L0_attn_w1"], f32))
    shared["w2_p"] = pack_w_outmajor(np.asarray(d["L0_attn_w2"], f32))
    shared["wq_p"] = pack_attn_outmajor(np.asarray(d["L0_attn_wq"], f32))
    shared["wk_p"] = pack_attn_outmajor(np.asarray(d["L0_attn_wk"], f32))
    shared["wv_p"] = pack_attn_outmajor(np.asarray(d["L0_attn_wv"], f32))
    shared["Wq_p"] = pack_attn_inmajor(np.asarray(d["L1_attn_Wq"], f32))
    shared["Wk_p"] = pack_attn_inmajor(np.asarray(d["L1_attn_Wk"], f32))
    shared["Wv_p"] = pack_attn_inmajor(np.asarray(d["L1_attn_Wv"], f32))
    wo = np.asarray(d["L1_attn_Wo"], f32)
    wo_e = np.zeros((2048, 1024), f32)
    wo_e.reshape(16, 128, 1024)[:, 0:64, :] = wo.reshape(16, 64, 1024)
    shared["Wo_p"] = pack_w_inmajor(wo_e)
    shared["fc1_p"] = pack_w_inmajor(np.asarray(d["L1_ffn_fc1_q"], np.int32))
    shared["fc2_p"] = pack_w_inmajor(np.asarray(d["L1_ffn_fc2_q"], np.int32))
    fc1s = np.ascontiguousarray(
        np.asarray(d["L1_ffn_fc1_scales"], f32).reshape(8, 32, 128)
        .transpose(1, 0, 2).reshape(32, 1024))
    fc2s = np.ascontiguousarray(
        np.asarray(d["L1_ffn_fc2_scales"], f32).reshape(32, 8, 128)
        .transpose(1, 0, 2).reshape(8, 4096))
    shared["fc1s_p"] = np.ascontiguousarray(
        np.broadcast_to(fc1s[:, None, :], (32, 128, 1024)))
    shared["fc2s_p"] = np.ascontiguousarray(
        np.broadcast_to(fc2s[:, None, :], (8, 128, 4096)))

    vecs = np.zeros((128, 160), f32)
    vecs[:, 0:8] = pack_vec(np.asarray(d["L0_attn_g1"], f32))
    vecs[:, 8:16] = pack_vec(np.asarray(d["L0_attn_be1"], f32))
    vecs[:, 16:24] = pack_vec(np.asarray(d["L0_attn_g2"], f32))
    vecs[:, 24:32] = pack_vec(np.asarray(d["L0_attn_be2"], f32))
    vecs[:, 32:64] = pack_vec(np.asarray(d["L0_attn_b1"], f32))
    vecs[:, 64:72] = pack_vec(np.asarray(d["L0_attn_b2"], f32))
    vecs[:, 72:80] = pack_vec(np.asarray(d["L1_ffn_ln_gamma"], f32))
    vecs[:, 80:88] = pack_vec(np.asarray(d["L1_ffn_ln_beta"], f32))
    vecs[:, 88:120] = pack_vec(np.asarray(d["L1_ffn_fc1_bias"], f32))
    vecs[:, 120:128] = pack_vec(np.asarray(d["L1_ffn_fc2_bias"], f32))
    vecs[:, 128:136] = pack_vec(np.asarray(d["L1_attn_bq"], f32))
    vecs[:, 136:144] = pack_vec(np.asarray(d["L1_attn_bk"], f32))
    shared["vecs"] = vecs

    rowsin = np.zeros((1, 1024), f32)
    rowsin[0, 0:1024] = np.asarray(d["L1_attn_bv"], f32)
    shared["rowsin"] = rowsin

    cp = np.zeros(16, f32)
    cp[0], cp[1], cp[2], cp[3], cp[4] = 1e-5, 1e-8, 1e-6, EXP_SHIFT, 0.125
    cp[5], cp[6], cp[7], cp[8], cp[9] = 0.25, 0.5, 0.0, 1.0, -128.0
    cp[10], cp[11], cp[12], cp[13] = 1.0 / D, -1.0 / D, 1.0 / FF, -1.0 / FF
    shared["cpool"] = np.tile(cp[None, :], (128, 1))
    shared["identf"] = np.eye(128, dtype=f32)
    shared["identb"] = np.eye(128, dtype=ml_dtypes.bfloat16)
    shared["onesf"] = np.ones((128, 128), f32)
    shared["onesb"] = np.ones((128, 128), ml_dtypes.bfloat16)
    shared["maskb"] = np.triu(np.ones((128, 128), ml_dtypes.bfloat16))

    x = np.asarray(d["L0_attn_x"], f32)
    in_maps = []
    for c in range(NCORES):
        xc = x[SEQ_PER_CORE * c: SEQ_PER_CORE * (c + 1)].reshape(T, D)
        xT = np.ascontiguousarray(xc.reshape(T, DC, 128).transpose(1, 2, 0))
        m = dict(shared)
        m["xT"] = xT
        in_maps.append(m)

    nc = build_nc(tuple(STAGES))
    res = run_bass_kernel_spmd(nc, in_maps, core_ids=list(range(NCORES)))
    out = np.empty((B, S, D), f32)
    for c in range(NCORES):
        yt = np.asarray(res.results[c]["yT"])
        xc = yt.transpose(2, 0, 1).reshape(T, D)
        out[SEQ_PER_CORE * c: SEQ_PER_CORE * (c + 1)] = xc.reshape(SEQ_PER_CORE, S, D)
    return out
